# version 08
# baseline (speedup 1.0000x reference)
import numpy as np
import ml_dtypes
import concourse.bass as bass
import concourse.mybir as mybir
from concourse.bass_utils import run_bass_kernel_spmd
from contextlib import ExitStack

F32 = mybir.dt.float32
BF16 = mybir.dt.bfloat16
AF = mybir.ActivationFunctionType
ALU = mybir.AluOpType
AXX = mybir.AxisListType.X

D = 1024
NT = 2048
NCT = 8
NH = 16
HD = 64
CH = 128
NCH = NT // CH
TC = 1024
NL, NE, FF = 4, 4, 4096
ENGS = ("tensor", "vector", "scalar", "sync", "gpsimd")


class Rec:

    def __init__(self, nc, stack):
        self.nc = nc
        self.stack = stack
        self.progs = {e: [] for e in ENGS}
        self.sems = {}
        self.cnt = {}
        self.sec = "init"
        self.tensor_labels = []

    def sem(self, name):
        if name not in self.sems:
            self.sems[name] = self.stack.enter_context(self.nc.semaphore(name))
            self.cnt[name] = 0
        return self.sems[name]

    def n(self, name):
        self.sem(name)
        return self.cnt[name]

    def e(self, eng, op, *args, inc=None, incn=1, **kw):
        if inc is not None:
            s = self.sem(inc)
            self.cnt[inc] += incn
            inc = (s, incn)
        if eng == "tensor" and op == "matmul":
            self.tensor_labels.append(self.sec)
        self.progs[eng].append((op, args, kw, inc))

    def fence(self, eng):
        lst = self.progs[eng]
        op, args, kw, inci = lst[-1]
        name = "fz_" + eng
        if inci is None:
            s = self.sem(name)
            self.cnt[name] += 1
            lst[-1] = (op, args, kw, (s, 1))
            self.w(eng, name, self.cnt[name])
        else:
            lst.append(("wait_ge", (inci[0], self.cnt_of(inci[0])), {}, None))

    def cnt_of(self, sem_handle):
        for k, v in self.sems.items():
            if v is sem_handle:
                return self.cnt[k]
        raise KeyError

    def w(self, eng, name, val):
        assert val <= self.n(name), (
            f"wait {name}>={val} but only {self.cnt[name]} emitted")
        if val > 0:
            self.progs[eng].append(("wait_ge", (self.sems[name], val), {},
                                    None))


def build_program(stage="full"):
    p0slow = stage == "p0slow"
    if p0slow:
        stage = "p0"
    nlayers = {"p0": 0, "l1": 1, "p1": NL}.get(stage, NL)
    nc = bass.Bass("TRN2", target_bir_lowering=False,
                   detect_race_conditions=False)
    dr = lambda n, s, d=F32, k="ExternalInput": nc.dram_tensor(
        n, s, d, kind=k).ap()

    x_tm = dr("x_tm", [NT, D])
    x_cm = dr("x_cm", [D, NT])
    wq_h = dr("wq_t", [128, NCT * D])
    wk_h = dr("wk_t", [128, NCT * D])
    wv_h = dr("wv_t", [128, NCT * D])
    wo_h = dr("wo_t", [128, NCT * D])
    wg1_h = dr("wg1_t", [128, NCT * HD])
    wg2_h = dr("wg2", [128, D])
    nw_h = dr("nw_t", [128, 8])
    rw_h = dr("rw_t", [128, 128])
    rbb_h = dr("rbb", [128, 128])
    b1t_h = dr("b1t", [128, 512])
    b2t_h = dr("b2t", [128, 128])
    lng_h = dr("lng", [128, 128])
    lnb_h = dr("lnb", [128, 128])
    w1_h = dr("w1", [NL, NE, D, FF])
    w2_h = dr("w2", [NL, NE, FF, D])
    cos_h = dr("cosb", [D, NT], BF16)
    sin_h = dr("sinb", [D, NT], BF16)
    idnf_h = dr("idnf", [128, 128])
    idnb_h = dr("idnb", [128, 128], BF16)
    mask_h = dr("mask4", [128, 512], BF16)
    oneh_h = dr("oneh4", [128, 512], BF16)
    out_h = dr("out", [D, NT], k="ExternalOutput")

    stack = ExitStack()
    with stack:
        R = Rec(nc, stack)
        sb = lambda n, s, d=F32: stack.enter_context(nc.sbuf_tensor(n, s, d))
        pb = [stack.enter_context(nc.psum_tensor(f"pb{i}", [128, 512], F32))
              for i in range(8)]

        idnf_s = sb("idnf_s", [128, 128])
        idnb_s = sb("idnb_s", [128, 128], BF16)
        mask_s = sb("mask_s", [128, 512], BF16)
        oneh_s = sb("oneh_s", [128, 512], BF16)
        onesb_s = sb("onesb_s", [128, 128], BF16)
        onesf_s = sb("onesf_s", [128, 128])
        nw_s = sb("nw_s", [128, 8])
        eps_s = sb("eps_s", [128, 2])

        for dst, src in [(idnf_s, idnf_h), (idnb_s, idnb_h), (mask_s, mask_h),
                         (oneh_s, oneh_h), (nw_s, nw_h)]:
            R.e("sync", "dma_start", dst[:], src[:], inc="cld", incn=16)
        R.e("vector", "memset", onesb_s[:], 1.0)
        R.e("vector", "memset", eps_s[:, 0:1], 1e-5)
        R.e("vector", "memset", eps_s[:, 1:2], 1e-6)
        R.e("vector", "memset", onesf_s[:], 1.0, inc="init")

        B_t = sb("B_arena", [128, 35456])

        class Carver:
            def __init__(self):
                self.off = 0

            def take(self, n_f32, dt=F32, parts=None):
                o = self.off
                self.off += n_f32
                assert self.off <= 35456, f"arena overflow {self.off}"
                ap = B_t[:, o:o + n_f32]
                if dt != F32:
                    ap = ap.bitcast(dt)
                if parts is not None:
                    ap = ap[parts[0]:parts[1], :]
                return ap

        aX = ExitStack()
        x_ffn = aX.enter_context(nc.sbuf_tensor("x_ffn", [128, NCT * NT], F32))
        xfv = x_ffn[:].rearrange("p (ct t) -> p ct t", ct=NCT)

        def emit_ln_tile(par, xsrc, xdst, cols, sq, ld_sem, ld_val, pf0,
                         sfx=""):
            pf = pf0 + sfx
            xn_sem = pf0 + "xn" + sfx
            xt = xsrc[:, par * D:(par + 1) * D]
            prev_var = R.n(pf + "var")
            prev_xn = R.n(xn_sem)
            R.w("vector", ld_sem, ld_val)
            R.e("vector", "reduce_sum", cols[:, 0:1], xt, axis=AXX)
            R.fence("vector")
            R.w("scalar", ld_sem, ld_val)
            R.w("scalar", pf + "var", prev_var)
            R.e("scalar", "activation", sq[:], xt, AF.Square,
                accum_out=cols[:, 1:2], inc=pf + "sq")
            R.e("vector", "tensor_scalar_mul", cols[:, 2:3], cols[:, 0:1],
                1.0 / D)
            R.fence("vector")
            R.e("vector", "tensor_mul", cols[:, 3:4], cols[:, 2:3],
                cols[:, 2:3])
            R.fence("vector")
            R.w("vector", pf + "sq", R.n(pf + "sq"))
            R.e("vector", "scalar_tensor_tensor", cols[:, 4:5], cols[:, 1:2],
                1.0 / D, cols[:, 3:4], ALU.mult, ALU.subtract, inc=pf + "var")
            R.w("scalar", pf + "var", R.n(pf + "var"))
            R.w("scalar", xn_sem, prev_xn)
            R.e("scalar", "activation", cols[:, 4:5], cols[:, 4:5], AF.Ln,
                bias=eps_s[:, 0:1])
            R.fence("scalar")
            R.e("scalar", "activation", cols[:, 5:6], cols[:, 4:5], AF.Exp,
                scale=-0.5, inc=pf + "rs")
            R.w("vector", pf + "rs", R.n(pf + "rs"))
            R.e("vector", "tensor_scalar",
                xdst[:, par * D:(par + 1) * D], xt, cols[:, 2:3], cols[:, 5:6],
                ALU.subtract, ALU.mult, inc=xn_sem)

        c0 = Carver()
        x0_s = c0.take(4 * D)
        xn0_s = c0.take(4 * D)
        sq0_s = c0.take(D)
        cols0_s = c0.take(32)

        R.w("tensor", "cld", R.n("cld"))
        R.w("scalar", "init", 1)
        R.sec = "p0_tp"
        for tt in range(16):
            par = tt % 2
            if tt >= 2:
                R.w("sync", "p0xn", tt - 1)
            if p0slow and tt >= 1:
                R.w("sync", "p0ev", 2 * tt)
            R.e("sync", "dma_start", x0_s[:, par * D:(par + 1) * D],
                x_tm[tt * 128:(tt + 1) * 128, :], inc=f"p0ld{par}", incn=16)
            if tt >= 2:
                R.w("vector", "p0mm", tt - 1)
            emit_ln_tile(par, x0_s, xn0_s, cols0_s, sq0_s,
                         f"p0ld{par}", 16 * (tt // 2 + 1), "p0")
            bA, bB = (pb[0], pb[1]) if par == 0 else (pb[2], pb[3])
            R.w("tensor", "p0xn", tt + 1)
            if tt >= 2:
                R.w("tensor", "p0ev", 2 * (tt - 1))
            for ct in range(NCT):
                bank = bA if ct < 4 else bB
                R.e("tensor", "matmul",
                    bank[:, (ct % 4) * 128:(ct % 4 + 1) * 128],
                    xn0_s[:, par * D + ct * 128:par * D + (ct + 1) * 128],
                    idnf_s[:], start=True, stop=True,
                    inc=("p0mm" if ct == NCT - 1 else None))
            R.w("vector", "p0mm", tt + 1)
            for half in range(2):
                bank = (bA, bB)[half]
                R.e("vector", "tensor_copy",
                    xfv[:, half * 4:(half + 1) * 4, tt * 128:(tt + 1) * 128],
                    bank[:].rearrange("p (ct t) -> p ct t", ct=4), inc="p0ev")
        p0ev_total = R.n("p0ev")

        c1 = Carver()
        h_s = c1.take(16 * TC, BF16)
        xhat_s = c1.take(NCT * TC // 2, BF16)
        xbe_s = c1.take(NCT * TC // 2, BF16)
        w1b_s = c1.take(3072, BF16)
        w2b_s = c1.take(4096, BF16)
        combb_s = c1.take(512, BF16)
        _w4 = c1.take(512)
        comb4_s = _w4.bitcast(BF16)[0:4, :]
        m3row_s = _w4.bitcast(BF16)[32:33, :]
        rsrow_s = _w4.bitcast(BF16)[64:65, :]
        logit_s = c1.take(32)
        scA_s = c1.take(8)
        scB_s = c1.take(8)
        rank_s = c1.take(32)
        m_s = c1.take(32)
        u_s = c1.take(32)
        usum_s = c1.take(8)
        rs_s = c1.take(8)
        combf_s = c1.take(32)
        combbf_s = c1.take(16, BF16)
        tb_s = c1.take(256, BF16)
        b1t_s = c1.take(512)
        b2t_s = c1.take(128)
        lng_s = c1.take(128)
        lnb_s = c1.take(128)
        rwb_s = c1.take(64, BF16)
        rbb_s = c1.take(128)
        rows_s = c1.take(1024, parts=(0, 1))

        for dst, src in [(b1t_s, b1t_h), (b2t_s, b2t_h), (lng_s, lng_h),
                         (lnb_s, lnb_h), (rbb_s, rbb_h)]:
            R.e("sync", "dma_start", dst[:], src[:], inc="cld2", incn=16)
        R.e("gpsimd", "dma_start", rwb_s[:], rw_h[:], inc="rwld", incn=16)
        cld_total = R.n("cld")

        for eng in ("vector", "scalar", "tensor"):
            R.w(eng, "p0ev", p0ev_total)
            R.w(eng, "cld2", R.n("cld2"))
        R.w("tensor", "rwld", 16)
        R.w("gpsimd", "init", 1)

        w1gate = []
        w2gate = []
        cbc = 0
        dbgx_h = None
        if stage == "l1":
            dbgx_h = nc.dram_tensor("d_xbf", [128, NCT * TC], BF16,
                                    kind="ExternalOutput").ap()

        for l in range(nlayers):
            for chunk in range(2):
                xsl = lambda kt, off=0, w=TC: x_ffn[
                    :, kt * NT + chunk * TC + off:
                    kt * NT + chunk * TC + off + w]
                lc = l * 2 + chunk
                hs_chunk0 = R.n("mmh")
                ys_chunk0 = R.n("mmy")
                R.w("vector", "mmh", R.n("mmh"))
                R.w("vector", "yadd", R.n("yadd"))
                for kt in range(NCT):
                    R.e("vector", "tensor_copy",
                        xbe_s[:, kt * TC:(kt + 1) * TC], xsl(kt),
                        inc=("xbf" if kt == NCT - 1 else None))
                if dbgx_h is not None and lc == 1:
                    R.w("sync", "xbf", R.n("xbf"))
                    R.e("sync", "dma_start", dbgx_h[:, :], xbe_s[:, :],
                        inc="dxf", incn=16)
                R.sec = f"lg{lc}"
                R.w("tensor", "xbf", R.n("xbf"))
                R.w("tensor", "lgev", R.n("lgev"))
                R.w("tensor", "yev", R.n("yev"))
                R.w("tensor", "xhat", R.n("xhat"))
                for tt in range(8):
                    bank = pb[6 + tt % 2]
                    c0 = (tt // 2) * 4
                    for kt in range(NCT):
                        R.e("tensor", "matmul", bank[:, c0:c0 + 4],
                            xbe_s[:, kt * TC + tt * 128:
                                  kt * TC + (tt + 1) * 128],
                            rwb_s[:, (l * 8 + kt) * 4:(l * 8 + kt + 1) * 4],
                            start=(kt == 0), stop=(kt == NCT - 1),
                            inc=("lgmm" if tt == 7 and kt == NCT - 1
                                 else None))
                R.w("vector", "lgmm", R.n("lgmm"))
                lgv4 = logit_s[:].rearrange("p (tt2 two e) -> p tt2 two e",
                                            two=2, e=4)
                R.e("vector", "tensor_copy", lgv4[:, :, 0, :],
                    pb[6][:, 0:16].rearrange("p (tt2 e) -> p tt2 e", e=4))
                R.e("vector", "tensor_copy", lgv4[:, :, 1, :],
                    pb[7][:, 0:16].rearrange("p (tt2 e) -> p tt2 e", e=4),
                    inc="lgev")
                R.w("vector", "lgev", R.n("lgev"))
                R.e("vector", "tensor_add", logit_s[:], logit_s[:],
                    rbb_s[:, l * 32:(l + 1) * 32])
                R.fence("vector")
                lgv = logit_s[:].rearrange("p (tt e) -> p tt e", e=4)
                rkv = rank_s[:].rearrange("p (tt e) -> p tt e", e=4)
                for e2 in range(4):
                    js = [j for j in range(4) if j != e2]
                    terms = [(j, ALU.is_gt) for j in js] + \
                        [(j, ALU.is_equal) for j in range(e2)]
                    R.e("vector", "tensor_tensor", scA_s[:],
                        lgv[:, :, terms[0][0]], lgv[:, :, e2], terms[0][1])
                    for ti, (j, op) in enumerate(terms[1:]):
                        R.e("vector", "tensor_tensor", scB_s[:],
                            lgv[:, :, j], lgv[:, :, e2], op)
                        R.fence("vector")
                        last = ti == len(terms) - 2
                        R.e("vector", "tensor_add",
                            rkv[:, :, e2] if last else scA_s[:],
                            scA_s[:], scB_s[:])
                R.fence("vector")
                R.e("vector", "tensor_single_scalar", m_s[:], rank_s[:], 1.5,
                    ALU.is_lt, inc="rnk")
                R.w("scalar", "rnk", R.n("rnk"))
                R.e("scalar", "activation", u_s[:], logit_s[:], AF.Exp,
                    inc="uexp")
                R.w("vector", "uexp", R.n("uexp"))
                R.e("vector", "tensor_mul", u_s[:], u_s[:], m_s[:])
                R.fence("vector")
                uv = u_s[:].rearrange("p (tt e) -> p tt e", e=4)
                R.e("vector", "tensor_add", usum_s[:], uv[:, :, 0],
                    uv[:, :, 1])
                R.e("vector", "tensor_add", scA_s[:], uv[:, :, 2], uv[:, :, 3])
                R.fence("vector")
                R.e("vector", "tensor_add", usum_s[:], usum_s[:], scA_s[:])
                R.fence("vector")
                R.e("vector", "reciprocal", rs_s[:], usum_s[:])
                R.fence("vector")
                cvv = combf_s[:].rearrange("p (tt e) -> p tt e", e=4)
                for e2 in range(4):
                    R.e("vector", "tensor_mul", cvv[:, :, e2], uv[:, :, e2],
                        rs_s[:])
                R.fence("vector")
                R.e("vector", "tensor_copy", combbf_s[:], combf_s[:],
                    inc="cbf")
                R.sec = f"ctp{lc}"
                R.w("tensor", "cbf", R.n("cbf"))
                R.w("tensor", "hge", R.n("hge"))
                for tt in range(8):
                    bank = pb[0] if tt < 4 else pb[1]
                    R.e("tensor", "matmul",
                        bank[0:4, (tt % 4) * 128:(tt % 4 + 1) * 128],
                        combbf_s[:, tt * 4:(tt + 1) * 4], idnb_s[:],
                        start=True, stop=True,
                        inc=("ctp" if tt == 7 else None))
                R.w("vector", "ctp", R.n("ctp"))
                for half in range(2):
                    R.e("vector", "tensor_copy",
                        comb4_s[0:4, half * 512:(half + 1) * 512],
                        pb[half][0:4, 0:512], inc="ctpe")
                R.sec = f"stat{lc}"
                R.w("scalar", "mmy", R.n("mmy"))
                R.w("scalar", "yadd", R.n("yadd"))
                xq = h_s[:, 0:NCT * TC]
                for kt in range(NCT):
                    R.e("scalar", "activation", xq[:, kt * TC:(kt + 1) * TC],
                        xsl(kt), AF.Square,
                        inc=("xsq" if kt == NCT - 1 else None))
                R.w("tensor", "xsq", R.n("xsq"))
                R.w("tensor", "lgev", R.n("lgev"))
                for kt in range(NCT):
                    for half in range(2):
                        R.e("tensor", "matmul", pb[4 + half][0:1, 0:512],
                            onesb_s[:, 0:1],
                            xbe_s[:, kt * TC + half * 512:
                                  kt * TC + (half + 1) * 512],
                            start=(kt == 0), stop=(kt == NCT - 1))
                        R.e("tensor", "matmul", pb[6 + half][0:1, 0:512],
                            onesb_s[:, 0:1],
                            xq[:, kt * TC + half * 512:
                               kt * TC + (half + 1) * 512],
                            start=(kt == 0), stop=(kt == NCT - 1),
                            inc=("stmm" if kt == NCT - 1 and half == 1
                                 else None))
                R.w("vector", "stmm", R.n("stmm"))
                for half in range(2):
                    mu_sl = rows_s[0:1, half * 512:(half + 1) * 512]
                    sc_sl = rows_s[0:1, (1 - half) * 512:(2 - half) * 512]
                    R.e("vector", "tensor_scalar_mul", mu_sl,
                        pb[4 + half][0:1, 0:512], 1.0 / D)
                    R.e("vector", "tensor_mul", sc_sl, mu_sl, mu_sl)
                    R.e("vector", "scalar_tensor_tensor", sc_sl,
                        pb[6 + half][0:1, 0:512], 1.0 / D, sc_sl,
                        ALU.mult, ALU.subtract, inc="rowm")
                    R.w("scalar", "rowm", R.n("rowm"))
                    R.e("scalar", "activation", sc_sl, sc_sl, AF.Ln,
                        bias=eps_s[0:1, 0:1])
                    R.fence("scalar")
                    R.e("scalar", "activation", sc_sl, sc_sl, AF.Exp,
                        scale=-0.5, inc="rowr")
                    R.w("vector", "rowr", R.n("rowr"))
                    R.e("vector", "tensor_copy",
                        rsrow_s[0:1, half * 512:(half + 1) * 512], sc_sl)
                    R.e("vector", "tensor_mul", mu_sl, mu_sl, sc_sl)
                    R.e("vector", "tensor_copy",
                        m3row_s[0:1, half * 512:(half + 1) * 512], mu_sl,
                        inc="rowb")
                R.sec = f"bcast{lc}"
                R.w("tensor", "rowb", R.n("rowb"))
                for half in range(2):
                    R.e("tensor", "matmul", pb[4 + half][:, 0:512],
                        onesb_s[32:33, 0:128],
                        m3row_s[0:1, half * 512:(half + 1) * 512],
                        start=True, stop=True)
                    R.e("tensor", "matmul", pb[6 + half][:, 0:512],
                        onesb_s[64:65, 0:128],
                        rsrow_s[0:1, half * 512:(half + 1) * 512],
                        start=True, stop=True,
                        inc=("bcm" if half == 1 else None))
                R.w("vector", "bcm", R.n("bcm"))
                for kt in range(NCT):
                    for half in range(2):
                        dst = xhat_s[:, kt * TC + half * 512:
                                     kt * TC + (half + 1) * 512]
                        R.e("vector", "tensor_mul", dst,
                            xsl(kt, half * 512, 512), pb[6 + half][:, 0:512])
                        R.e("vector", "tensor_sub", dst, dst,
                            pb[4 + half][:, 0:512],
                            inc=("xhat" if kt == NCT - 1 and half == 1
                                 else None))
                def emit_xe(e_):
                    le_ = l * NE + e_
                    R.w("vector", "lgmm", R.n("lgmm"))
                    R.w("vector", "stmm", R.n("stmm"))
                    R.w("vector", "mmh", R.n("mmh"))
                    R.w("vector", "xhat", R.n("xhat"))
                    if dbgx_h is not None:
                        R.w("vector", "dxf", R.n("dxf"))
                    for kt in range(NCT):
                        R.e("vector", "tensor_scalar",
                            xbe_s[:, kt * TC:(kt + 1) * TC],
                            xhat_s[:, kt * TC:(kt + 1) * TC],
                            lng_s[:, le_ * 8 + kt:le_ * 8 + kt + 1],
                            lnb_s[:, le_ * 8 + kt:le_ * 8 + kt + 1],
                            ALU.mult, ALU.add,
                            inc=("xe" if kt == NCT - 1 else None))

                emit_xe(0)
                for e in range(NE):
                    le = l * NE + e
                    R.sec = f"h{lc}e{e}"
                    R.w("tensor", "xe", R.n("xe"))
                    hs_e0 = R.n("mmh")
                    for fb in range(16):
                        g1i = len(w1gate)
                        if g1i >= 3:
                            R.w("gpsimd", "mmh", w1gate[g1i - 3])
                        R.e("gpsimd", "dma_start",
                            w1b_s[:, (g1i % 3) * 2048:(g1i % 3 + 1) * 2048]
                            .rearrange("p (kt f) -> p kt f", f=256),
                            w1_h[l, e, :, fb * 256:(fb + 1) * 256]
                            .rearrange("(kt p) f -> p kt f", p=128),
                            inc=f"w1ld{g1i % 2}", incn=16)
                        R.w("tensor", f"w1ld{g1i % 2}",
                            16 * (g1i // 2 + 1))
                        for sub in range(4):
                            ftl, half = sub // 2, sub % 2
                            hs = R.n("mmh")
                            bank = pb[hs % 4]
                            if hs >= 4:
                                R.w("tensor", "hge", hs - 3)
                            if hs - hs_e0 < 4:
                                R.w("tensor", "ctpe", R.n("ctpe"))
                                R.w("tensor", "cbe", R.n("cbe"))
                            for kt in range(NCT):
                                R.e("tensor", "matmul", bank[:, 0:512],
                                    w1b_s[:, (g1i % 3) * 2048 + kt * 256
                                          + ftl * 128:(g1i % 3) * 2048
                                          + kt * 256 + ftl * 128 + 128],
                                    xbe_s[:, kt * TC + half * 512:
                                          kt * TC + (half + 1) * 512],
                                    start=(kt == 0), stop=(kt == NCT - 1),
                                    inc=("mmh" if kt == NCT - 1 else None))
                            ft = fb * 2 + ftl
                            R.w("scalar", "mmh", R.n("mmh"))
                            R.e("scalar", "activation",
                                h_s[:, ft * TC + half * 512:
                                    ft * TC + (half + 1) * 512],
                                bank[:, 0:512], AF.Gelu,
                                bias=b1t_s[:, le * 32 + ft:le * 32 + ft + 1],
                                inc="hge")
                        w1gate.append(R.n("mmh"))
                    R.sec = f"cb{lc}e{e}"
                    R.w("tensor", "ctpe", R.n("ctpe"))
                    R.w("tensor", "hge", R.n("hge"))
                    for th in range(2):
                        bank = pb[2 + cbc % 2]
                        if cbc >= 2:
                            R.w("tensor", "cbe", cbc - 1)
                        R.e("tensor", "matmul", bank[:, 0:512],
                            oneh_s[0:4, e * 128:(e + 1) * 128],
                            comb4_s[0:4, th * 512:(th + 1) * 512],
                            start=True, stop=True, inc="cbm")
                        R.w("vector", "cbm", R.n("cbm"))
                        R.e("vector", "tensor_copy",
                            combb_s[:, th * 512:(th + 1) * 512],
                            bank[:, 0:512], inc="cbe")
                        cbc += 1
                    if e + 1 < NE:
                        emit_xe(e + 1)
                    R.sec = f"y{lc}e{e}"
                    for ht in range(8):
                        for fh in range(2):
                            g2i = len(w2gate) + fh
                            if g2i >= 4:
                                R.w("gpsimd", "mmy", w2gate[g2i - 4])
                            R.e("gpsimd", "dma_start",
                                w2b_s[:, (g2i % 4) * 2048:
                                      (g2i % 4 + 1) * 2048]
                                .rearrange("p (kt h) -> p kt h", h=128),
                                w2_h[l, e, fh * 2048:(fh + 1) * 2048,
                                     ht * 128:(ht + 1) * 128]
                                .rearrange("(kt p) h -> p kt h", p=128),
                                inc=f"w2ld{g2i % 3}", incn=16)
                        for fh2 in range(2):
                            gi2 = len(w2gate) + fh2
                            R.w("tensor", f"w2ld{gi2 % 3}",
                                16 * (gi2 // 3 + 1))
                        slot0 = len(w2gate) % 4
                        slot1 = (len(w2gate) + 1) % 4
                        R.w("tensor", "hge", R.n("hge"))
                        for th in range(2):
                            ys = R.n("mmy")
                            bank = pb[4 + ys % 4]
                            if ys >= 4:
                                R.w("tensor", "yev", ys - 3)
                            if ys - ys_chunk0 < 4:
                                R.w("tensor", "xhat", R.n("xhat"))
                            for kt in range(32):
                                slot = slot0 if kt < 16 else slot1
                                R.e("tensor", "matmul", bank[:, 0:512],
                                    w2b_s[:, slot * 2048 + (kt % 16) * 128:
                                          slot * 2048 + (kt % 16) * 128
                                          + 128],
                                    h_s[:, kt * TC + th * 512:
                                        kt * TC + (th + 1) * 512],
                                    start=(kt == 0), stop=(kt == 31),
                                    inc=("mmy" if kt == 31 else None))
                            R.w("vector", "mmy", R.n("mmy"))
                            R.e("vector", "scalar_tensor_tensor", tb_s[:],
                                bank[:, 0:512],
                                b2t_s[:, le * 8 + ht:le * 8 + ht + 1],
                                combb_s[:, th * 512:(th + 1) * 512],
                                ALU.add, ALU.mult, inc="yev")
                            R.e("vector", "tensor_add",
                                xsl(ht, th * 512, 512),
                                xsl(ht, th * 512, 512), tb_s[:], inc="yadd")
                        w2gate.append(R.n("mmy"))
                        w2gate.append(R.n("mmy"))

        yadd_total = R.n("yadd")
        mmy_total = R.n("mmy")

        if stage in ("p0", "l1", "p1"):
            R.w("sync", "yadd", yadd_total)
            R.w("sync", "p0ev", p0ev_total)
            R.e("sync", "dma_start",
                out_h.rearrange("(ct p) t -> p ct t", p=128),
                x_ffn[:].rearrange("p (ct t) -> p ct t", ct=NCT),
                inc="st0", incn=16)
            if stage == "l1":
                dbg_defs = [("d_rwb", rwb_s, [128, 128], BF16),
                            ("d_xhat", xhat_s, [128, 8192], BF16),
                            ("d_logit", logit_s, [128, 32], F32),
                            ("d_rank", rank_s, [128, 32], F32),
                            ("d_m", m_s, [128, 32], F32),
                            ("d_comb", combf_s, [128, 32], F32),
                            ("d_u", u_s, [128, 32], F32),
                            ("d_combb", combb_s, [128, 1024], BF16),
                            ("d_xbe", xbe_s, [128, 8192], BF16)]
                for nm, src_ap, shp, dt_ in dbg_defs:
                    hdl = nc.dram_tensor(nm, shp, dt_,
                                         kind="ExternalOutput").ap()
                    R.e("sync", "dma_start", hdl[:, :], src_ap[:, :],
                        inc="st0", incn=16)
            R.w("sync", "st0", R.n("st0"))
            with nc.Block() as block:
                def mk(eng_name):
                    def body(eng):
                        for op, args, kw, inci in R.progs[eng_name]:
                            ins = getattr(eng, op)(*args, **kw)
                            if inci is not None:
                                ins.then_inc(*inci)
                    return body
                block.sync(mk("sync"))
                block.tensor(mk("tensor"))
                block.vector(mk("vector"))
                block.scalar(mk("scalar"))
                block.gpsimd(mk("gpsimd"))
            aX.close()
            return nc

        c2 = Carver()
        g_s = c2.take(NCT * NT // 2, BF16)
        gv = g_s.rearrange("p (ct t) -> p ct t", ct=NCT)
        R.w("scalar", "yadd", yadd_total)
        R.w("scalar", "mmy", mmy_total)
        for kt in range(2 * NCT):
            R.e("scalar", "activation", g_s[:, kt * TC:(kt + 1) * TC],
                x_ffn[:, kt * TC:(kt + 1) * TC], AF.Silu,
                inc=("gsl" if kt == 2 * NCT - 1 else None))
        gsl_total = R.n("gsl")
        aX.close()

        xn_s = stack.enter_context(
            nc.sbuf_tensor("xn_s", [128, NCT * NT], BF16))
        wq_s = stack.enter_context(
            nc.sbuf_tensor("wq_s", [128, NCT * D], BF16))
        wk_s = stack.enter_context(
            nc.sbuf_tensor("wk_s", [128, NCT * D], BF16))
        wv_s = c2.take(NCT * D // 2, BF16)
        wo_s = c2.take(NCT * D // 2, BF16)
        wg1_s = c2.take(NCT * HD // 2, BF16)
        wg2_s = c2.take(512, BF16)
        cosb_s = c2.take(1024, BF16)
        sinb_s = c2.take(1024, BF16)
        xcm_s = c2.take(2 * 1024)
        outb_s = c2.take(2 * 1024)
        E_s = c2.take(512, BF16)
        T_s = c2.take(1024)
        qs_s = c2.take(1024)
        sig_s = c2.take(1024, BF16)
        sigq_s = c2.take(512, BF16)
        Qcos_s = c2.take(512, BF16)
        Qsin_s = c2.take(512, BF16)
        Kcos_s = c2.take(512, BF16)
        Ksin_s = c2.take(512, BF16)
        KcosT_s = c2.take(512, BF16)
        KsinT_s = c2.take(512, BF16)
        V_s = c2.take(512, BF16)
        A_s = c2.take(1024, BF16)
        gg_s = c2.take(1024)
        ms_s = c2.take(16)
        rstd_s = c2.take(16)
        g1_s = c2.take(64, BF16, parts=(0, 64))
        normed_s = c2.take(512, BF16)
        carry_s = c2.take(8)
        Scf_s = c2.take(512)
        Ssf_s = c2.take(512)
        Scb_s = c2.take(256, BF16)
        Ssb_s = c2.take(256, BF16)
        cols2_s = c2.take(32)
        gg2_s = qs_s
        x2_s = outb_s
        xn2_s = A_s


        R.w("vector", "yadd", yadd_total)
        R.w("vector", "gsl", gsl_total)
        R.e("vector", "memset", carry_s[:], 1.0)
        R.e("vector", "memset", Scf_s[:], 0.0)
        R.e("vector", "memset", Ssf_s[:], 0.0)
        R.e("vector", "memset", Scb_s[:], 0.0)
        R.e("vector", "memset", Ssb_s[:], 0.0, inc="init2")

        R.w("gpsimd", "yadd", yadd_total)
        R.w("gpsimd", "mmy", mmy_total)
        R.w("gpsimd", "gsl", gsl_total)
        for dst, src in [(wq_s, wq_h), (wk_s, wk_h), (wv_s, wv_h),
                         (wg1_s, wg1_h), (wg2_s, wg2_h), (wo_s, wo_h)]:
            R.e("gpsimd", "dma_start", dst[:], src[:], inc="awl", incn=16)
        awl_total = R.n("awl")
        R.w("vector", "awl", awl_total)
        for kt in range(NCT):
            R.e("vector", "tensor_scalar_mul", wo_s[:, kt * D:(kt + 1) * D],
                wo_s[:, kt * D:(kt + 1) * D], nw_s[:, kt:kt + 1], inc="wof")

        R.sec = "p2_ln"
        R.w("sync", "gsl", gsl_total)
        xnv = xn_s[:].rearrange("p (ct t) -> p ct t", ct=NCT)
        x2bufs = [outb_s, xcm_s]
        for tt in range(16):
            par = tt % 4
            x2c = x2bufs[par // 2]
            if tt >= 4:
                R.w("sync", f"p2xn{par}", tt // 4)
            R.e("sync", "dma_start",
                x2c[:, (par % 2) * D:(par % 2 + 1) * D],
                x_tm[tt * 128:(tt + 1) * 128, :], inc=f"p2ld{par}", incn=16)
            if tt >= 2:
                R.w("vector", "p2mm", tt - 1)
            emit_ln_tile(par % 2, x2c, xn2_s,
                         cols2_s[:, par * 8:par * 8 + 8], gg_s[:, 0:1024],
                         f"p2ld{par}", 16 * (tt // 4 + 1), "p2",
                         sfx=str(par))
            bA, bB = (pb[0], pb[1]) if tt % 2 == 0 else (pb[2], pb[3])
            R.w("tensor", f"p2xn{par}", tt // 4 + 1)
            if tt == 0:
                R.w("tensor", "hge", R.n("hge"))
                R.w("tensor", "yev", R.n("yev"))
                R.w("tensor", "xhat", R.n("xhat"))
                R.w("tensor", "lgev", R.n("lgev"))
                R.w("tensor", "cbe", R.n("cbe"))
            if tt >= 2:
                R.w("tensor", "p2ev", 2 * (tt - 1))
            for ct in range(NCT):
                bank = bA if ct < 4 else bB
                R.e("tensor", "matmul",
                    bank[:, (ct % 4) * 128:(ct % 4 + 1) * 128],
                    xn2_s[:, (par % 2) * D + ct * 128:
                          (par % 2) * D + (ct + 1) * 128],
                    idnb_s[:], start=True, stop=True,
                    inc=("p2mm" if ct == NCT - 1 else None))
            R.w("vector", "p2mm", tt + 1)
            for half in range(2):
                bank = (bA, bB)[half]
                R.e("vector", "tensor_copy",
                    xnv[:, half * 4:(half + 1) * 4, tt * 128:(tt + 1) * 128],
                    bank[:].rearrange("p (ct t) -> p ct t", ct=4), inc="p2ev")
        p2ev_total = R.n("p2ev")

        tpc = 0
        R.w("tensor", "awl", awl_total)
        R.w("tensor", "wof", R.n("wof"))
        R.w("tensor", "p2ev", p2ev_total)
        R.w("tensor", "init2", 1)

        def tp_batch(lhs_src, cols, dst, dst_off, nb):
            nonlocal tpc
            bank = pb[6 + tpc % 2]
            if tpc >= 2:
                R.w("tensor", "tpe", tpc - 1)
            for j in range(nb):
                R.e("tensor", "matmul", bank[:, j * 128:(j + 1) * 128],
                    lhs_src[:, cols[j] * 128:(cols[j] + 1) * 128], idnb_s[:],
                    start=True, stop=True,
                    inc=("tpm" if j == nb - 1 else None))
            R.w("vector", "tpm", R.n("tpm"))
            R.e("vector", "tensor_copy", dst[:, dst_off:dst_off + nb * 128],
                bank[:, 0:nb * 128], inc="tpe")
            tpc += 1

        def emitA(i):
            par = i % 2
            tok = i * CH
            if i >= 2:
                R.w("sync", "dvq", i - 1)
            R.e("sync", "dma_start",
                cosb_s[:, par * 1024:(par + 1) * 1024]
                .rearrange("p (ct t) -> p ct t", ct=NCT),
                cos_h.rearrange("(ct p) t -> p ct t", p=128)[:, :,
                                                            tok:tok + CH],
                inc=f"cs{par}", incn=16)
            R.e("sync", "dma_start",
                sinb_s[:, par * 1024:(par + 1) * 1024]
                .rearrange("p (ct t) -> p ct t", ct=NCT),
                sin_h.rearrange("(ct p) t -> p ct t", p=128)[:, :,
                                                            tok:tok + CH],
                inc=f"cs{par}", incn=16)
            if i >= 2:
                R.w("sync", "oev", i - 1)
            else:
                R.w("sync", "p2xn2", 4)
                R.w("sync", "p2xn3", 4)
            R.e("sync", "dma_start",
                xcm_s[:, par * 1024:(par + 1) * 1024]
                .rearrange("p (ct t) -> p ct t", ct=NCT),
                x_cm.rearrange("(ct p) t -> p ct t", p=128)[:, :,
                                                           tok:tok + CH],
                inc=f"xld{par}", incn=16)

            R.sec = f"kq{i}"
            R.w("tensor", "sgev", 2 * i)
            for ct in range(NCT):
                bank = pb[ct % 2]
                sl = (ct // 2) * 128
                for kt in range(NCT):
                    R.e("tensor", "matmul", bank[:, sl:sl + 128],
                        wk_s[:, kt * D + ct * 128:kt * D + (ct + 1) * 128],
                        xn_s[:, kt * NT + tok:kt * NT + tok + CH],
                        start=(kt == 0), stop=(kt == NCT - 1),
                        inc=("kmm" if ct == NCT - 1 and kt == NCT - 1
                             else None))
            R.w("scalar", "kmm", i + 1)
            R.w("scalar", "dvq", i)
            E_v = E_s.rearrange("p (c2 two t) -> p c2 two t", two=2, t=128)
            R.e("scalar", "activation", E_v[:, :, 0, :],
                pb[0][:, 0:512].rearrange("p (c2 t) -> p c2 t", t=128),
                AF.Exp)
            R.e("scalar", "activation", E_v[:, :, 1, :],
                pb[1][:, 0:512].rearrange("p (c2 t) -> p c2 t", t=128),
                AF.Exp, inc="eev")
            R.w("tensor", "eev", i + 1)
            for ct in range(NCT):
                bank = pb[ct % 2]
                sl = (ct // 2) * 128
                for kt in range(NCT):
                    R.e("tensor", "matmul", bank[:, sl:sl + 128],
                        wq_s[:, kt * D + ct * 128:kt * D + (ct + 1) * 128],
                        xn_s[:, kt * NT + tok:kt * NT + tok + CH],
                        start=(kt == 0), stop=(kt == NCT - 1),
                        inc=("qmm" if ct == NCT - 1 and kt == NCT - 1
                             else None))
            R.w("vector", "eev", i + 1)
            if i == 0:
                R.w("vector", "init2", 1)
            for ct in range(NCT):
                R.e("vector", "tensor_tensor_scan",
                    T_s[:, ct * 128:(ct + 1) * 128], onesf_s[:],
                    E_s[:, ct * 128:(ct + 1) * 128], carry_s[:, ct:ct + 1],
                    ALU.mult, ALU.add)
            R.e("vector", "tensor_copy", carry_s[:], T_s[:, 127::128],
                inc="scn")
            R.w("scalar", "scn", R.n("scn"))
            R.e("scalar", "activation", T_s[:], T_s[:], AF.Ln)
            R.e("scalar", "activation", T_s[:], T_s[:], AF.Exp, scale=-1.0,
                inc="rte")
            R.w("scalar", "qmm", i + 1)
            R.e("scalar", "activation", sigq_s[:, 0:512], pb[0][:, 0:512],
                AF.Sigmoid)
            R.e("scalar", "activation", sigq_s[:, 512:1024], pb[1][:, 0:512],
                AF.Sigmoid, inc="sqe")
            R.w("vector", "sqe", i + 1)
            qs_v = qs_s.rearrange("p (c2 two t) -> p c2 two t", two=2, t=128)
            R.e("vector", "tensor_mul", qs_v[:, :, 0, :],
                pb[0][:, 0:512].rearrange("p (c2 t) -> p c2 t", t=128),
                sigq_s[:, 0:512].rearrange("p (c2 t) -> p c2 t", t=128))
            R.e("vector", "tensor_mul", qs_v[:, :, 1, :],
                pb[1][:, 0:512].rearrange("p (c2 t) -> p c2 t", t=128),
                sigq_s[:, 512:1024].rearrange("p (c2 t) -> p c2 t", t=128))
            R.w("vector", "rte", i + 1)
            R.e("vector", "tensor_mul", qs_s[:], qs_s[:], T_s[:], inc="qse")
            R.sec = f"vg{i}"
            R.w("tensor", "qse", i + 1)
            for half in range(2):
                for kt in range(NCT):
                    R.e("tensor", "matmul", pb[half][:, 0:512],
                        xn_s[:, kt * NT + tok:kt * NT + tok + CH],
                        wv_s[:, kt * D + half * 512:
                             kt * D + (half + 1) * 512],
                        start=(kt == 0), stop=(kt == NCT - 1),
                        inc=("vmm" if kt == NCT - 1 else None))
            R.w("vector", "vmm", R.n("vmm"))
            R.e("vector", "tensor_copy", V_s[:, 0:512], pb[0][:, 0:512])
            R.e("vector", "tensor_copy", V_s[:, 512:1024], pb[1][:, 0:512],
                inc="vev")
            R.w("tensor", "vev", i + 1)
            for kt in range(NCT):
                R.e("tensor", "matmul", pb[0][0:64, 0:128],
                    wg1_s[:, kt * HD:(kt + 1) * HD],
                    xn_s[:, kt * NT + tok:kt * NT + tok + CH],
                    start=(kt == 0), stop=(kt == NCT - 1),
                    inc=("g1mm" if kt == NCT - 1 else None))
            R.w("vector", "g1mm", i + 1)
            R.e("vector", "tensor_copy", g1_s[:], pb[0][0:64, 0:128],
                inc="g1ev")
            R.w("tensor", "g1ev", i + 1)
            for half in range(2):
                R.e("tensor", "matmul", pb[half][:, 0:512], g1_s[:],
                    wg2_s[0:64, half * 512:(half + 1) * 512],
                    start=True, stop=True,
                    inc=("gtmm" if half == 1 else None))
            R.w("scalar", "gtmm", i + 1)
            R.e("scalar", "activation", sig_s[:, par * 1024:par * 1024 + 512],
                pb[0][:, 0:512], AF.Sigmoid)
            R.e("scalar", "activation",
                sig_s[:, par * 1024 + 512:par * 1024 + 1024],
                pb[1][:, 0:512], AF.Sigmoid, inc="sgev", incn=2)

        def emitB_head(i):
            nonlocal tpc
            par = i % 2
            R.w("vector", f"cs{par}", 32 * (i // 2 + 1))
            cbuf = cosb_s[:, par * 1024:(par + 1) * 1024]
            sbf = sinb_s[:, par * 1024:(par + 1) * 1024]
            R.e("vector", "tensor_mul", Qcos_s[:], qs_s[:], cbuf)
            R.e("vector", "tensor_mul", Qsin_s[:], qs_s[:], sbf)
            R.e("vector", "tensor_mul", Kcos_s[:], E_s[:], cbuf)
            R.e("vector", "tensor_mul", Ksin_s[:], E_s[:], sbf, inc="dvq")
            R.sec = f"tpA{i}"
            R.w("tensor", "dvq", i + 1)
            tp_batch(Kcos_s, [0, 1, 2, 3], KcosT_s, 0, 4)
            tp_batch(Kcos_s, [4, 5, 6, 7], KcosT_s, 512, 4)
            tp_batch(Ksin_s, [0, 1, 2, 3], KsinT_s, 0, 4)
            tp_batch(Ksin_s, [4, 5, 6, 7], KsinT_s, 512, 4)
            ktpe_done = R.n("tpe")
            A_v = A_s.rearrange("p (hh two t) -> p hh two t", two=2, t=128)
            mask_v = mask_s[:].rearrange("p (m t) -> p m t", t=128)
            for octave in range(2):
                cA, cB = tpc, tpc + 1
                if cA >= 2:
                    R.w("tensor", "tpe", cA - 1)
                if cB >= 2:
                    R.w("tensor", "tpe", cB - 1)
                for j in range(8):
                    h = 8 * octave + j
                    bank = pb[6 + j % 2]
                    sl = (j // 2) * 128
                    p0 = 64 * (h % 2)
                    c0 = (h // 2) * 128
                    R.e("tensor", "matmul", bank[:, sl:sl + 128],
                        Kcos_s[p0:p0 + 64, c0:c0 + 128],
                        Qcos_s[p0:p0 + 64, c0:c0 + 128],
                        start=True, stop=False)
                    R.e("tensor", "matmul", bank[:, sl:sl + 128],
                        Ksin_s[p0:p0 + 64, c0:c0 + 128],
                        Qsin_s[p0:p0 + 64, c0:c0 + 128],
                        start=False, stop=True,
                        inc=("tpm" if j >= 6 else None))
                tbase = R.n("tpm")
                R.w("vector", "tpm", tbase - 1)
                R.e("vector", "tensor_mul",
                    A_v[:, 4 * octave:4 * octave + 4, 0, :],
                    pb[6][:, 0:512].rearrange("p (m t) -> p m t", t=128),
                    mask_v, inc="tpe")
                R.w("vector", "tpm", tbase)
                R.e("vector", "tensor_mul",
                    A_v[:, 4 * octave:4 * octave + 4, 1, :],
                    pb[7][:, 0:512].rearrange("p (m t) -> p m t", t=128),
                    mask_v, inc="tpe")
                tpc += 2
            aev_done = R.n("tpe")
            R.sec = f"dS{i}"
            R.w("tensor", "tpe", ktpe_done)
            R.w("tensor", "scst", 2 * i)
            for h in range(NH):
                p0 = 64 * (h % 2)
                kc0 = (h // 2) * 128 + 64 * (h % 2)
                sc0 = (h // 2) * 64
                R.e("tensor", "matmul", pb[4][p0:p0 + 64, sc0:sc0 + 64],
                    KcosT_s[:, kc0:kc0 + 64], V_s[:, h * 64:(h + 1) * 64],
                    start=True, stop=True)
                R.e("tensor", "matmul", pb[5][p0:p0 + 64, sc0:sc0 + 64],
                    KsinT_s[:, kc0:kc0 + 64], V_s[:, h * 64:(h + 1) * 64],
                    start=True, stop=True,
                    inc=("dmm" if h == NH - 1 else None))
            R.sec = f"o{i}"
            R.w("tensor", "tpe", aev_done)
            R.w("tensor", "oev", i)
            for jj in range(NH):
                h = (jj // 2) + 8 * (jj % 2)
                p0 = 64 * (h % 2)
                c0 = (h // 2) * 128
                sc0 = (h // 2) * 64
                ob = pb[2 + h // 8]
                oc = (h % 8) * 64
                R.e("tensor", "matmul", ob[:, oc:oc + 64],
                    A_s[:, h * 128:(h + 1) * 128],
                    V_s[:, h * 64:(h + 1) * 64], start=True, stop=False)
                R.e("tensor", "matmul", ob[:, oc:oc + 64],
                    Qcos_s[p0:p0 + 64, c0:c0 + 128],
                    Scb_s[p0:p0 + 64, sc0:sc0 + 64], start=False, stop=False)
                R.e("tensor", "matmul", ob[:, oc:oc + 64],
                    Qsin_s[p0:p0 + 64, c0:c0 + 128],
                    Ssb_s[p0:p0 + 64, sc0:sc0 + 64], start=False, stop=True,
                    inc=("omm" if jj == NH - 1 else None))
            R.w("vector", "dmm", i + 1)
            R.e("vector", "tensor_add", Scf_s[:], Scf_s[:], pb[4][:, 0:512])
            R.e("vector", "tensor_add", Ssf_s[:], Ssf_s[:], pb[5][:, 0:512])
            R.w("vector", "omm", i + 1)
            R.e("vector", "tensor_copy", Scb_s[:], Scf_s[:])
            R.e("vector", "tensor_copy", Ssb_s[:], Ssf_s[:], inc="scst",
                incn=2)
            R.w("vector", "sgev", 2 * (i + 1))
            R.e("vector", "tensor_mul", gg_s[:, 0:512], pb[2][:, 0:512],
                sig_s[:, par * 1024:par * 1024 + 512])
            R.e("vector", "tensor_mul", gg_s[:, 512:1024], pb[3][:, 0:512],
                sig_s[:, par * 1024 + 512:par * 1024 + 1024],
                inc="ggv", incn=2)
            R.w("scalar", "ggv", R.n("ggv"))
            R.e("scalar", "activation", gg2_s[:], gg_s[:], AF.Square,
                inc="g2e")
            R.w("vector", "g2e", R.n("g2e"))
            for h in range(NH):
                R.e("vector", "reduce_sum", ms_s[:, h:h + 1],
                    gg2_s[:, h * 64:(h + 1) * 64], axis=AXX,
                    inc=("msv" if h == NH - 1 else None))
            R.w("scalar", "msv", R.n("msv"))
            R.e("scalar", "activation", ms_s[:], ms_s[:], AF.Ln,
                scale=1.0 / HD, bias=eps_s[:, 1:2])
            R.fence("scalar")
            R.e("scalar", "activation", rstd_s[:], ms_s[:], AF.Exp,
                scale=-0.5, inc="rse")
            R.w("vector", "rse", i + 1)
            for h in range(NH):
                R.e("vector", "tensor_single_scalar",
                    normed_s[:, h * 64:(h + 1) * 64],
                    gg_s[:, h * 64:(h + 1) * 64], rstd_s[:, h:h + 1],
                    ALU.mult, inc=("nrm" if h == NH - 1 else None))

        def emitB_tail(i):
            par = i % 2
            tok = i * CH
            R.sec = f"ntp{i}"
            R.w("tensor", "nrm", i + 1)
            tp_batch(normed_s, [0, 1, 2, 3], A_s, 0, 4)
            tp_batch(normed_s, [4, 5, 6, 7], A_s, 512, 4)
            R.sec = f"sp{i}"
            R.w("tensor", "tpe", R.n("tpe"))
            R.w("tensor", "ggv", R.n("ggv"))
            for ct in range(NCT):
                bank = pb[2 + ct % 2]
                sl = (ct // 2) * 128
                for kt in range(NCT):
                    R.e("tensor", "matmul", bank[:, sl:sl + 128],
                        wo_s[:, kt * D + ct * 128:kt * D + (ct + 1) * 128],
                        A_s[:, kt * 128:(kt + 1) * 128],
                        start=(kt == 0), stop=(kt == NCT - 1),
                        inc=("smm" if ct == NCT - 1 and kt == NCT - 1
                             else None))
            R.w("vector", "smm", i + 1)
            R.w("vector", f"xld{par}", 16 * (i // 2 + 1))
            if i >= 2:
                R.w("vector", f"st{par}", 16 * ((i - 2) // 2 + 1))
            gg_v = gg_s.rearrange("p (c2 two t) -> p c2 two t", two=2,
                                  t=128)
            for half in range(2):
                R.e("vector", "tensor_mul", gg_v[:, :, half, :],
                    pb[2 + half][:, 0:512]
                    .rearrange("p (ct t) -> p ct t", ct=4),
                    gv[:, half::2, tok:tok + CH])
            R.e("vector", "tensor_add",
                outb_s[:, par * 1024:(par + 1) * 1024], gg_s[:],
                xcm_s[:, par * 1024:(par + 1) * 1024], inc="oev")
            R.w("sync", "oev", i + 1)
            R.e("sync", "dma_start",
                out_h.rearrange("(ct p) t -> p ct t", p=128)[:, :,
                                                            tok:tok + CH],
                outb_s[:, par * 1024:(par + 1) * 1024]
                .rearrange("p (ct t) -> p ct t", ct=NCT),
                inc=f"st{par}", incn=16)

        emitA(0)
        for i in range(NCH):
            emitB_head(i)
            if i + 1 < NCH:
                emitA(i + 1)
            emitB_tail(i)

        R.w("sync", "st0", 16 * (NCH // 2))
        R.w("sync", "st1", 16 * (NCH // 2))

        globals()["TLABELS"] = list(R.tensor_labels)
        with nc.Block() as block:
            def mk(eng_name):
                def body(eng):
                    for op, args, kw, inci in R.progs[eng_name]:
                        ins = getattr(eng, op)(*args, **kw)
                        if inci is not None:
                            ins.then_inc(*inci)
                return body
            block.sync(mk("sync"))
            block.tensor(mk("tensor"))
            block.vector(mk("vector"))
            block.scalar(mk("scalar"))
            block.gpsimd(mk("gpsimd"))
    return nc


def host_prep(inputs):
    f32 = np.float32
    wq = np.asarray(inputs["L0_attn_wq"], f32)
    wk = np.asarray(inputs["L0_attn_wk"], f32)
    wv = np.asarray(inputs["L0_attn_wv"], f32)
    wo = np.asarray(inputs["L0_attn_wo"], f32)
    wg1 = np.asarray(inputs["L0_attn_wg1"], f32)
    wg2 = np.asarray(inputs["L0_attn_wg2"], f32)
    nw = np.asarray(inputs["L0_attn_norm_w"], f32)
    rw = np.asarray(inputs["L0_ffn_router_w"], f32)
    rb = np.asarray(inputs["L0_ffn_router_b"], f32)
    lng = np.asarray(inputs["L0_ffn_ln_g"], f32)
    lnb = np.asarray(inputs["L0_ffn_ln_b"], f32)
    w1 = np.asarray(inputs["L0_ffn_w1"], f32)
    b1 = np.asarray(inputs["L0_ffn_b1"], f32)
    w2 = np.asarray(inputs["L0_ffn_w2"], f32)
    b2 = np.asarray(inputs["L0_ffn_b2"], f32)

    wt = lambda w: np.ascontiguousarray(
        w.reshape(8, 128, -1).transpose(1, 0, 2).reshape(128, -1))
    common = {
        "wq_t": wt(wq), "wk_t": wt(wk), "wv_t": wt(wv), "wo_t": wt(wo),
        "wg1_t": wt(wg1),
        "wg2": np.ascontiguousarray(
            np.concatenate([wg2, np.zeros((64, D), np.float32)], axis=0)),
        "nw_t": np.ascontiguousarray(nw.reshape(8, 128).T),
        "rw_t": np.ascontiguousarray(
            rw.reshape(NL, 8, 128, 4).transpose(2, 0, 1, 3).reshape(128,
                                                                    128)),
        "rbb": np.ascontiguousarray(np.broadcast_to(
            np.broadcast_to(rb.reshape(NL, 1, 4), (NL, 8, 4)).reshape(1, 128),
            (128, 128))),
        "b1t": np.ascontiguousarray(
            b1.reshape(NL, NE, 32, 128).transpose(3, 0, 1, 2).reshape(128,
                                                                      512)),
        "b2t": np.ascontiguousarray(
            b2.reshape(NL, NE, 8, 128).transpose(3, 0, 1, 2).reshape(128,
                                                                     128)),
        "lng": np.ascontiguousarray(
            lng.reshape(NL, NE, 8, 128).transpose(3, 0, 1, 2).reshape(128,
                                                                      128)),
        "lnb": np.ascontiguousarray(
            lnb.reshape(NL, NE, 8, 128).transpose(3, 0, 1, 2).reshape(128,
                                                                      128)),
        "w1": w1, "w2": w2,
        "idnf": np.eye(128, dtype=f32),
        "idnb": np.eye(128, dtype=ml_dtypes.bfloat16),
    }
    th = 10000.0 ** (-2.0 / D * np.arange(D, dtype=np.float64))
    ang = th[:, None] * np.arange(NT, dtype=np.float64)[None, :]
    common["cosb"] = np.cos(ang).astype(ml_dtypes.bfloat16)
    common["sinb"] = np.sin(ang).astype(ml_dtypes.bfloat16)
    tri = (np.arange(128)[:, None] <= np.arange(128)[None, :])
    common["mask4"] = np.ascontiguousarray(
        np.tile(tri, (1, 4)).astype(ml_dtypes.bfloat16))
    oneh = np.zeros((128, 512), dtype=ml_dtypes.bfloat16)
    for e in range(4):
        oneh[e, e * 128:(e + 1) * 128] = 1.0
    common["oneh4"] = oneh
    return common


def kernel(**inputs):
    x = np.asarray(inputs["L0_attn_x"], dtype=np.float32)
    B = x.shape[0]
    common = host_prep(inputs)
    nc = build_program()
    in_maps = []
    for b in range(B):
        m = dict(common)
        m["x_tm"] = np.ascontiguousarray(x[b])
        m["x_cm"] = np.ascontiguousarray(x[b].T)
        in_maps.append(m)
    res = run_bass_kernel_spmd(nc, in_maps, core_ids=list(range(B)))
    out = np.stack([res.results[b]["out"].T for b in range(B)], axis=0)
    return out.astype(np.float32)


# version 09 (final)
# speedup vs baseline: 1.1737x; 1.1737x over previous
import numpy as np
import ml_dtypes
import concourse.bass as bass
import concourse.mybir as mybir
from concourse.bass_utils import run_bass_kernel_spmd
from contextlib import ExitStack

F32 = mybir.dt.float32
BF16 = mybir.dt.bfloat16
AF = mybir.ActivationFunctionType
ALU = mybir.AluOpType
AXX = mybir.AxisListType.X

D = 1024
NT = 2048
NCT = 8
NH = 16
HD = 64
CH = 128
NCH = NT // CH
TC = 1024
NL, NE, FF = 4, 4, 4096
ENGS = ("tensor", "vector", "scalar", "sync", "gpsimd")


class Rec:

    def __init__(self, nc, stack):
        self.nc = nc
        self.stack = stack
        self.progs = {e: [] for e in ENGS}
        self.sems = {}
        self.cnt = {}
        self.sec = "init"
        self.tensor_labels = []

    def sem(self, name):
        if name not in self.sems:
            self.sems[name] = self.stack.enter_context(self.nc.semaphore(name))
            self.cnt[name] = 0
        return self.sems[name]

    def n(self, name):
        self.sem(name)
        return self.cnt[name]

    def e(self, eng, op, *args, inc=None, incn=1, **kw):
        if inc is not None:
            s = self.sem(inc)
            self.cnt[inc] += incn
            inc = (s, incn)
        if eng == "tensor" and op == "matmul":
            self.tensor_labels.append(self.sec)
        self.progs[eng].append((op, args, kw, inc))

    def fence(self, eng):
        lst = self.progs[eng]
        op, args, kw, inci = lst[-1]
        name = "fz_" + eng
        if inci is None:
            s = self.sem(name)
            self.cnt[name] += 1
            lst[-1] = (op, args, kw, (s, 1))
            self.w(eng, name, self.cnt[name])
        else:
            lst.append(("wait_ge", (inci[0], self.cnt_of(inci[0])), {}, None))

    def cnt_of(self, sem_handle):
        for k, v in self.sems.items():
            if v is sem_handle:
                return self.cnt[k]
        raise KeyError

    def w(self, eng, name, val):
        assert val <= self.n(name), (
            f"wait {name}>={val} but only {self.cnt[name]} emitted")
        if val > 0:
            self.progs[eng].append(("wait_ge", (self.sems[name], val), {},
                                    None))


def build_program(stage="full"):
    p0slow = stage == "p0slow"
    if p0slow:
        stage = "p0"
    nlayers = {"p0": 0, "l1": 1, "p1": NL}.get(stage, NL)
    nc = bass.Bass("TRN2", target_bir_lowering=False,
                   detect_race_conditions=False)
    dr = lambda n, s, d=F32, k="ExternalInput": nc.dram_tensor(
        n, s, d, kind=k).ap()

    x_tm = dr("x_tm", [NT, D])
    x_cm = dr("x_cm", [D, NT])
    wq_h = dr("wq_t", [128, NCT * D])
    wk_h = dr("wk_t", [128, NCT * D])
    wv_h = dr("wv_t", [128, NCT * D])
    wo_h = dr("wo_t", [128, NCT * D])
    wg1_h = dr("wg1_t", [128, NCT * HD])
    wg2_h = dr("wg2", [128, D])
    nw_h = dr("nw_t", [128, 8])
    rw_h = dr("rw_t", [128, 128])
    rbb_h = dr("rbb", [128, 128])
    b1t_h = dr("b1t", [128, 512])
    b2t_h = dr("b2t", [128, 128])
    lng_h = dr("lng", [128, 128])
    lnb_h = dr("lnb", [128, 128])
    w1_h = dr("w1", [NL, NE, D, FF])
    w2_h = dr("w2", [NL, NE, FF, D])
    cos_h = dr("cosb", [D, NT], BF16)
    sin_h = dr("sinb", [D, NT], BF16)
    idnf_h = dr("idnf", [128, 128])
    idnb_h = dr("idnb", [128, 128], BF16)
    mask_h = dr("mask4", [128, 512], BF16)
    oneh_h = dr("oneh4", [128, 512], BF16)
    out_h = dr("out", [D, NT], k="ExternalOutput")

    stack = ExitStack()
    with stack:
        R = Rec(nc, stack)
        sb = lambda n, s, d=F32: stack.enter_context(nc.sbuf_tensor(n, s, d))
        pb = [stack.enter_context(nc.psum_tensor(f"pb{i}", [128, 512], F32))
              for i in range(8)]

        idnf_s = sb("idnf_s", [128, 128])
        idnb_s = sb("idnb_s", [128, 128], BF16)
        mask_s = sb("mask_s", [128, 512], BF16)
        oneh_s = sb("oneh_s", [128, 512], BF16)
        onesb_s = sb("onesb_s", [128, 128], BF16)
        onesf_s = sb("onesf_s", [128, 128])
        nw_s = sb("nw_s", [128, 8])
        eps_s = sb("eps_s", [128, 2])

        for dst, src in [(idnf_s, idnf_h), (idnb_s, idnb_h), (mask_s, mask_h),
                         (oneh_s, oneh_h), (nw_s, nw_h)]:
            R.e("sync", "dma_start", dst[:], src[:], inc="cld", incn=16)
        R.e("vector", "memset", onesb_s[:], 1.0)
        R.e("vector", "memset", eps_s[:, 0:1], 1e-5)
        R.e("vector", "memset", eps_s[:, 1:2], 1e-6)
        R.e("vector", "memset", onesf_s[:], 1.0, inc="init")

        B_t = sb("B_arena", [128, 35456])

        class Carver:
            def __init__(self):
                self.off = 0

            def take(self, n_f32, dt=F32, parts=None):
                o = self.off
                self.off += n_f32
                assert self.off <= 35456, f"arena overflow {self.off}"
                ap = B_t[:, o:o + n_f32]
                if dt != F32:
                    ap = ap.bitcast(dt)
                if parts is not None:
                    ap = ap[parts[0]:parts[1], :]
                return ap

        aX = ExitStack()
        x_ffn = aX.enter_context(nc.sbuf_tensor("x_ffn", [128, NCT * NT], F32))
        xfv = x_ffn[:].rearrange("p (ct t) -> p ct t", ct=NCT)

        def emit_ln_tile(par, xsrc, xdst, cols, sq, ld_sem, ld_val, pf0,
                         sfx=""):
            pf = pf0 + sfx
            xn_sem = pf0 + "xn" + sfx
            xt = xsrc[:, par * D:(par + 1) * D]
            prev_var = R.n(pf + "var")
            prev_xn = R.n(xn_sem)
            R.w("vector", ld_sem, ld_val)
            R.e("vector", "reduce_sum", cols[:, 0:1], xt, axis=AXX)
            R.fence("vector")
            R.w("scalar", ld_sem, ld_val)
            R.w("scalar", pf + "var", prev_var)
            R.e("scalar", "activation", sq[:], xt, AF.Square,
                accum_out=cols[:, 1:2], inc=pf + "sq")
            R.e("vector", "tensor_scalar_mul", cols[:, 2:3], cols[:, 0:1],
                1.0 / D)
            R.fence("vector")
            R.e("vector", "tensor_mul", cols[:, 3:4], cols[:, 2:3],
                cols[:, 2:3])
            R.fence("vector")
            R.w("vector", pf + "sq", R.n(pf + "sq"))
            R.e("vector", "scalar_tensor_tensor", cols[:, 4:5], cols[:, 1:2],
                1.0 / D, cols[:, 3:4], ALU.mult, ALU.subtract, inc=pf + "var")
            R.w("scalar", pf + "var", R.n(pf + "var"))
            R.w("scalar", xn_sem, prev_xn)
            R.e("scalar", "activation", cols[:, 4:5], cols[:, 4:5], AF.Ln,
                bias=eps_s[:, 0:1])
            R.fence("scalar")
            R.e("scalar", "activation", cols[:, 5:6], cols[:, 4:5], AF.Exp,
                scale=-0.5, inc=pf + "rs")
            R.w("vector", pf + "rs", R.n(pf + "rs"))
            R.e("vector", "tensor_scalar",
                xdst[:, par * D:(par + 1) * D], xt, cols[:, 2:3], cols[:, 5:6],
                ALU.subtract, ALU.mult, inc=xn_sem)

        c0 = Carver()
        x0_s = c0.take(4 * D)
        xn0_s = c0.take(4 * D)
        sq0_s = c0.take(D)
        cols0_s = c0.take(32)

        R.w("tensor", "cld", R.n("cld"))
        R.w("scalar", "init", 1)
        R.sec = "p0_tp"
        for tt in range(16):
            par = tt % 2
            if tt >= 2:
                R.w("sync", "p0xn", tt - 1)
            if p0slow and tt >= 1:
                R.w("sync", "p0ev", 2 * tt)
            R.e("sync", "dma_start", x0_s[:, par * D:(par + 1) * D],
                x_tm[tt * 128:(tt + 1) * 128, :], inc=f"p0ld{par}", incn=16)
            if tt >= 2:
                R.w("vector", "p0mm", tt - 1)
            emit_ln_tile(par, x0_s, xn0_s, cols0_s, sq0_s,
                         f"p0ld{par}", 16 * (tt // 2 + 1), "p0")
            bA, bB = (pb[0], pb[1]) if par == 0 else (pb[2], pb[3])
            R.w("tensor", "p0xn", tt + 1)
            if tt >= 2:
                R.w("tensor", "p0ev", 2 * (tt - 1))
            for ct in range(NCT):
                bank = bA if ct < 4 else bB
                R.e("tensor", "matmul",
                    bank[:, (ct % 4) * 128:(ct % 4 + 1) * 128],
                    xn0_s[:, par * D + ct * 128:par * D + (ct + 1) * 128],
                    idnf_s[:], start=True, stop=True,
                    inc=("p0mm" if ct == NCT - 1 else None))
            R.w("vector", "p0mm", tt + 1)
            for half in range(2):
                bank = (bA, bB)[half]
                R.e("vector", "tensor_copy",
                    xfv[:, half * 4:(half + 1) * 4, tt * 128:(tt + 1) * 128],
                    bank[:].rearrange("p (ct t) -> p ct t", ct=4), inc="p0ev")
        p0ev_total = R.n("p0ev")

        c1 = Carver()
        h_s = c1.take(16 * TC, BF16)
        xhat_s = c1.take(NCT * TC // 2, BF16)
        xbe_s = c1.take(NCT * TC // 2, BF16)
        w1b_s = c1.take(3072, BF16)
        w2b_s = c1.take(4096, BF16)
        combb_s = c1.take(512, BF16)
        _w4 = c1.take(512)
        comb4_s = _w4.bitcast(BF16)[0:4, :]
        m3row_s = _w4.bitcast(BF16)[32:33, :]
        rsrow_s = _w4.bitcast(BF16)[64:65, :]
        logit_s = c1.take(32)
        scA_s = c1.take(8)
        scB_s = c1.take(8)
        rank_s = c1.take(32)
        m_s = c1.take(32)
        u_s = c1.take(32)
        usum_s = c1.take(8)
        rs_s = c1.take(8)
        combf_s = c1.take(32)
        combbf_s = c1.take(16, BF16)
        tb_s = c1.take(256, BF16)
        b1t_s = c1.take(512)
        b2t_s = c1.take(128)
        lng_s = c1.take(128)
        lnb_s = c1.take(128)
        rwb_s = c1.take(64, BF16)
        rbb_s = c1.take(128)
        rows_s = c1.take(1024, parts=(0, 1))

        for dst, src in [(b1t_s, b1t_h), (b2t_s, b2t_h), (lng_s, lng_h),
                         (lnb_s, lnb_h), (rbb_s, rbb_h)]:
            R.e("sync", "dma_start", dst[:], src[:], inc="cld2", incn=16)
        R.e("gpsimd", "dma_start", rwb_s[:], rw_h[:], inc="rwld", incn=16)
        cld_total = R.n("cld")

        for eng in ("vector", "scalar", "tensor"):
            R.w(eng, "p0ev", p0ev_total)
            R.w(eng, "cld2", R.n("cld2"))
        R.w("tensor", "rwld", 16)
        R.w("gpsimd", "init", 1)

        w1gate = []
        w2gate = []
        cbc = 0
        dbgx_h = None
        if stage == "l1":
            dbgx_h = nc.dram_tensor("d_xbf", [128, NCT * TC], BF16,
                                    kind="ExternalOutput").ap()

        lcs = [(l_, c_) for l_ in range(nlayers) for c_ in range(2)]
        yadd_done = {}

        def xslf(chunk):
            return lambda kt, off=0, w=TC: x_ffn[
                :, kt * NT + chunk * TC + off:
                kt * NT + chunk * TC + off + w]

        def emit_xe(l, e_):
            le_ = l * NE + e_
            R.w("vector", "lgmm", R.n("lgmm"))
            R.w("vector", "stmm", R.n("stmm"))
            R.w("vector", "mmh", R.n("mmh"))
            R.w("vector", "xhat", R.n("xhat"))
            if dbgx_h is not None:
                R.w("vector", "dxf", R.n("dxf"))
            for kt in range(NCT):
                R.e("vector", "tensor_scalar",
                    xbe_s[:, kt * TC:(kt + 1) * TC],
                    xhat_s[:, kt * TC:(kt + 1) * TC],
                    lng_s[:, le_ * 8 + kt:le_ * 8 + kt + 1],
                    lnb_s[:, le_ * 8 + kt:le_ * 8 + kt + 1],
                    ALU.mult, ALU.add,
                    inc=("xe" if kt == NCT - 1 else None))

        def emit_pre_x(l, chunk):
            xsl = xslf(chunk)
            lc = l * 2 + chunk
            R.w("vector", "mmh", R.n("mmh"))
            R.w("vector", "yadd", yadd_done.get((l - 1, chunk), 0))
            for kt in range(NCT):
                R.e("vector", "tensor_copy",
                    xbe_s[:, kt * TC:(kt + 1) * TC], xsl(kt),
                    inc=("xbf" if kt == NCT - 1 else None))
            if dbgx_h is not None and lc == 1:
                R.w("sync", "xbf", R.n("xbf"))
                R.e("sync", "dma_start", dbgx_h[:, :], xbe_s[:, :],
                    inc="dxf", incn=16)

        def emit_pre_y1(l, chunk):
            xsl = xslf(chunk)
            lc = l * 2 + chunk
            R.sec = f"lg{lc}"
            R.w("tensor", "xbf", R.n("xbf"))
            R.w("tensor", "lgev", R.n("lgev"))
            R.w("tensor", "xhat", R.n("xhat"))
            for tt in range(8):
                bank = pb[2 + tt % 2]
                c0 = (tt // 2) * 4
                for kt in range(NCT):
                    R.e("tensor", "matmul", bank[:, c0:c0 + 4],
                        xbe_s[:, kt * TC + tt * 128:
                              kt * TC + (tt + 1) * 128],
                        rwb_s[:, (l * 8 + kt) * 4:(l * 8 + kt + 1) * 4],
                        start=(kt == 0), stop=(kt == NCT - 1),
                        inc=("lgmm" if tt == 7 and kt == NCT - 1
                             else None))
            R.sec = f"stat{lc}"
            R.w("scalar", "xe", R.n("xe"))
            R.w("scalar", "yadd", yadd_done.get((l - 1, chunk), 0))
            xq = xhat_s[:, 0:NCT * TC]
            for kt in range(NCT):
                R.e("scalar", "activation", xq[:, kt * TC:(kt + 1) * TC],
                    xsl(kt), AF.Square,
                    inc=("xsq" if kt == NCT - 1 else None))
            R.sec = f"lg{lc}"
            R.w("vector", "lgmm", R.n("lgmm"))
            lgv4 = logit_s[:].rearrange("p (tt2 two e) -> p tt2 two e",
                                        two=2, e=4)
            R.e("vector", "tensor_copy", lgv4[:, :, 0, :],
                pb[2][:, 0:16].rearrange("p (tt2 e) -> p tt2 e", e=4))
            R.e("vector", "tensor_copy", lgv4[:, :, 1, :],
                pb[3][:, 0:16].rearrange("p (tt2 e) -> p tt2 e", e=4),
                inc="lgev")
            R.w("vector", "lgev", R.n("lgev"))
            R.e("vector", "tensor_add", logit_s[:], logit_s[:],
                rbb_s[:, l * 32:(l + 1) * 32])
            R.fence("vector")
            lgv = logit_s[:].rearrange("p (tt e) -> p tt e", e=4)
            rkv = rank_s[:].rearrange("p (tt e) -> p tt e", e=4)
            for e2 in range(4):
                js = [j for j in range(4) if j != e2]
                terms = [(j, ALU.is_gt) for j in js] + \
                    [(j, ALU.is_equal) for j in range(e2)]
                R.e("vector", "tensor_tensor", scA_s[:],
                    lgv[:, :, terms[0][0]], lgv[:, :, e2], terms[0][1])
                for ti, (j, op) in enumerate(terms[1:]):
                    R.e("vector", "tensor_tensor", scB_s[:],
                        lgv[:, :, j], lgv[:, :, e2], op)
                    R.fence("vector")
                    last = ti == len(terms) - 2
                    R.e("vector", "tensor_add",
                        rkv[:, :, e2] if last else scA_s[:],
                        scA_s[:], scB_s[:])
            R.fence("vector")
            R.e("vector", "tensor_single_scalar", m_s[:], rank_s[:], 1.5,
                ALU.is_lt, inc="rnk")
            R.w("scalar", "rnk", R.n("rnk"))
            R.e("scalar", "activation", u_s[:], logit_s[:], AF.Exp,
                inc="uexp")
            R.w("vector", "uexp", R.n("uexp"))
            R.e("vector", "tensor_mul", u_s[:], u_s[:], m_s[:])
            R.fence("vector")
            uv = u_s[:].rearrange("p (tt e) -> p tt e", e=4)
            R.e("vector", "tensor_add", usum_s[:], uv[:, :, 0],
                uv[:, :, 1])
            R.e("vector", "tensor_add", scA_s[:], uv[:, :, 2], uv[:, :, 3])
            R.fence("vector")
            R.e("vector", "tensor_add", usum_s[:], usum_s[:], scA_s[:])
            R.fence("vector")
            R.e("vector", "reciprocal", rs_s[:], usum_s[:])
            R.fence("vector")
            cvv = combf_s[:].rearrange("p (tt e) -> p tt e", e=4)
            for e2 in range(4):
                R.e("vector", "tensor_mul", cvv[:, :, e2], uv[:, :, e2],
                    rs_s[:])
            R.fence("vector")
            R.e("vector", "tensor_copy", combbf_s[:], combf_s[:],
                inc="cbf")
            R.sec = f"stat{lc}"
            R.w("tensor", "xsq", R.n("xsq"))
            R.w("tensor", "lgev", R.n("lgev"))
            R.w("tensor", "ctpe", R.n("ctpe"))
            R.w("tensor", "cbe", R.n("cbe"))
            for kt in range(NCT):
                for half in range(2):
                    R.e("tensor", "matmul", pb[6 + half][0:1, 0:512],
                        onesb_s[:, 0:1],
                        xbe_s[:, kt * TC + half * 512:
                              kt * TC + (half + 1) * 512],
                        start=(kt == 0), stop=(kt == NCT - 1))
                    R.e("tensor", "matmul", pb[2 + half][0:1, 0:512],
                        onesb_s[:, 0:1],
                        xq[:, kt * TC + half * 512:
                           kt * TC + (half + 1) * 512],
                        start=(kt == 0), stop=(kt == NCT - 1),
                        inc=("stmm" if kt == NCT - 1 and half == 1
                             else None))
            R.w("vector", "stmm", R.n("stmm"))
            for half in range(2):
                mu_sl = rows_s[0:1, half * 512:(half + 1) * 512]
                sc_sl = rows_s[0:1, (1 - half) * 512:(2 - half) * 512]
                R.e("vector", "tensor_scalar_mul", mu_sl,
                    pb[6 + half][0:1, 0:512], 1.0 / D)
                R.e("vector", "tensor_mul", sc_sl, mu_sl, mu_sl)
                R.e("vector", "scalar_tensor_tensor", sc_sl,
                    pb[2 + half][0:1, 0:512], 1.0 / D, sc_sl,
                    ALU.mult, ALU.subtract, inc="rowm")
                R.w("scalar", "rowm", R.n("rowm"))
                R.e("scalar", "activation", sc_sl, sc_sl, AF.Ln,
                    bias=eps_s[0:1, 0:1])
                R.fence("scalar")
                R.e("scalar", "activation", sc_sl, sc_sl, AF.Exp,
                    scale=-0.5, inc="rowr")
                R.w("vector", "rowr", R.n("rowr"))
                R.e("vector", "tensor_copy",
                    rsrow_s[0:1, half * 512:(half + 1) * 512], sc_sl)
                R.e("vector", "tensor_mul", mu_sl, mu_sl, sc_sl)
                R.e("vector", "tensor_copy",
                    m3row_s[0:1, half * 512:(half + 1) * 512], mu_sl,
                    inc="rowb")

        def emit_pre_y2(l, chunk):
            xsl = xslf(chunk)
            lc = l * 2 + chunk
            R.sec = f"ctp{lc}"
            R.w("tensor", "cbf", R.n("cbf"))
            R.w("tensor", "rowb", R.n("rowb"))
            for tt in range(8):
                bank = pb[6] if tt < 4 else pb[7]
                R.e("tensor", "matmul",
                    bank[0:4, (tt % 4) * 128:(tt % 4 + 1) * 128],
                    combbf_s[:, tt * 4:(tt + 1) * 4], idnb_s[:],
                    start=True, stop=True,
                    inc=("ctp" if tt == 7 else None))
            R.w("vector", "ctp", R.n("ctp"))
            for half in range(2):
                R.e("vector", "tensor_copy",
                    comb4_s[0:4, half * 512:(half + 1) * 512],
                    pb[6 + half][0:4, 0:512], inc="ctpe")
            R.sec = f"bcast{lc}"
            R.w("tensor", "ctpe", R.n("ctpe"))
            for half in range(2):
                R.e("tensor", "matmul", pb[2 + half][:, 0:512],
                    onesb_s[32:33, 0:128],
                    m3row_s[0:1, half * 512:(half + 1) * 512],
                    start=True, stop=True)
                R.e("tensor", "matmul", pb[6 + half][:, 0:512],
                    onesb_s[64:65, 0:128],
                    rsrow_s[0:1, half * 512:(half + 1) * 512],
                    start=True, stop=True,
                    inc=("bcm" if half == 1 else None))
        def emit_xh_kt(l, chunk, kt):
            xsl = xslf(chunk)
            le_ = l * NE
            if kt == 0:
                R.w("vector", "bcm", R.n("bcm"))
                R.w("vector", "lgmm", R.n("lgmm"))
                R.w("vector", "stmm", R.n("stmm"))
                R.w("vector", "mmh", R.n("mmh"))
                if dbgx_h is not None:
                    R.w("vector", "dxf", R.n("dxf"))
            for half in range(2):
                dst = xhat_s[:, kt * TC + half * 512:
                             kt * TC + (half + 1) * 512]
                R.e("vector", "tensor_mul", dst,
                    xsl(kt, half * 512, 512), pb[6 + half][:, 0:512])
                R.e("vector", "tensor_sub", dst, dst,
                    pb[2 + half][:, 0:512],
                    inc=("xhat" if kt == NCT - 1 and half == 1
                         else None))
            R.e("vector", "tensor_scalar",
                xbe_s[:, kt * TC:(kt + 1) * TC],
                xhat_s[:, kt * TC:(kt + 1) * TC],
                lng_s[:, le_ * 8 + kt:le_ * 8 + kt + 1],
                lnb_s[:, le_ * 8 + kt:le_ * 8 + kt + 1],
                ALU.mult, ALU.add,
                inc=("xe" if kt == NCT - 1 else None))

        def emit_experts(idx):
            nonlocal cbc
            l, chunk = lcs[idx]
            xsl = xslf(chunk)
            lc = l * 2 + chunk
            nxt = lcs[idx + 1] if idx + 1 < len(lcs) else None
            for e in range(NE):
                le = l * NE + e
                R.sec = f"h{lc}e{e}"
                R.w("tensor", "xe", R.n("xe"))
                for fb in range(16):
                    g1i = len(w1gate)
                    if g1i >= 3:
                        R.w("gpsimd", "mmh", w1gate[g1i - 3])
                    R.e("gpsimd", "dma_start",
                        w1b_s[:, (g1i % 3) * 2048:(g1i % 3 + 1) * 2048]
                        .rearrange("p (kt f) -> p kt f", f=256),
                        w1_h[l, e, :, fb * 256:(fb + 1) * 256]
                        .rearrange("(kt p) f -> p kt f", p=128),
                        inc=f"w1ld{g1i % 2}", incn=16)
                    R.w("tensor", f"w1ld{g1i % 2}",
                        16 * (g1i // 2 + 1))
                    for sub in range(4):
                        ftl, half = sub // 2, sub % 2
                        hs = R.n("mmh")
                        bank = pb[hs % 2]
                        if hs >= 2:
                            R.w("tensor", "hge", hs - 1)
                        for kt in range(NCT):
                            R.e("tensor", "matmul", bank[:, 0:512],
                                w1b_s[:, (g1i % 3) * 2048 + kt * 256
                                      + ftl * 128:(g1i % 3) * 2048
                                      + kt * 256 + ftl * 128 + 128],
                                xbe_s[:, kt * TC + half * 512:
                                      kt * TC + (half + 1) * 512],
                                start=(kt == 0), stop=(kt == NCT - 1),
                                inc=("mmh" if kt == NCT - 1 else None))
                        ft = fb * 2 + ftl
                        R.w("scalar", "mmh", R.n("mmh"))
                        R.e("scalar", "activation",
                            h_s[:, ft * TC + half * 512:
                                ft * TC + (half + 1) * 512],
                            bank[:, 0:512], AF.Gelu,
                            bias=b1t_s[:, le * 32 + ft:le * 32 + ft + 1],
                            inc="hge")
                    w1gate.append(R.n("mmh"))
                R.sec = f"cb{lc}e{e}"
                R.w("tensor", "ctpe", R.n("ctpe"))
                R.w("tensor", "xhat", R.n("xhat"))
                for th in range(2):
                    bank = pb[6 + cbc % 2]
                    if cbc >= 2:
                        R.w("tensor", "cbe", cbc - 1)
                    R.e("tensor", "matmul", bank[:, 0:512],
                        oneh_s[0:4, e * 128:(e + 1) * 128],
                        comb4_s[0:4, th * 512:(th + 1) * 512],
                        start=True, stop=True, inc="cbm")
                    R.w("vector", "cbm", R.n("cbm"))
                    R.e("vector", "tensor_copy",
                        combb_s[:, th * 512:(th + 1) * 512],
                        bank[:, 0:512], inc="cbe")
                    cbc += 1
                if e + 1 < NE:
                    emit_xe(l, e + 1)
                elif nxt is not None:
                    emit_pre_x(*nxt)
                R.sec = f"y{lc}e{e}"
                for ht in range(8):
                    if e == NE - 1 and nxt is not None:
                        if ht == 2:
                            emit_pre_y1(*nxt)
                            R.sec = f"y{lc}e{e}"
                        elif ht == 4:
                            emit_pre_y2(*nxt)
                            R.sec = f"y{lc}e{e}"
                        if ht >= 4:
                            emit_xh_kt(*nxt, (ht - 4) * 2)
                            emit_xh_kt(*nxt, (ht - 4) * 2 + 1)
                            R.sec = f"y{lc}e{e}"
                    for fh in range(2):
                        g2i = len(w2gate) + fh
                        if g2i >= 4:
                            R.w("gpsimd", "mmy", w2gate[g2i - 4])
                        R.e("gpsimd", "dma_start",
                            w2b_s[:, (g2i % 4) * 2048:
                                  (g2i % 4 + 1) * 2048]
                            .rearrange("p (kt h) -> p kt h", h=128),
                            w2_h[l, e, fh * 2048:(fh + 1) * 2048,
                                 ht * 128:(ht + 1) * 128]
                            .rearrange("(kt p) h -> p kt h", p=128),
                            inc=f"w2ld{g2i % 3}", incn=16)
                    for fh2 in range(2):
                        gi2 = len(w2gate) + fh2
                        R.w("tensor", f"w2ld{gi2 % 3}",
                            16 * (gi2 // 3 + 1))
                    slot0 = len(w2gate) % 4
                    slot1 = (len(w2gate) + 1) % 4
                    R.w("tensor", "hge", R.n("hge"))
                    for th in range(2):
                        ys = R.n("mmy")
                        bank = pb[4 + ys % 2]
                        if ys >= 2:
                            R.w("tensor", "yev", ys - 1)
                        for kt in range(32):
                            slot = slot0 if kt < 16 else slot1
                            R.e("tensor", "matmul", bank[:, 0:512],
                                w2b_s[:, slot * 2048 + (kt % 16) * 128:
                                      slot * 2048 + (kt % 16) * 128
                                      + 128],
                                h_s[:, kt * TC + th * 512:
                                    kt * TC + (th + 1) * 512],
                                start=(kt == 0), stop=(kt == 31),
                                inc=("mmy" if kt == 31 else None))
                        R.w("vector", "mmy", R.n("mmy"))
                        R.e("vector", "scalar_tensor_tensor", tb_s[:],
                            bank[:, 0:512],
                            b2t_s[:, le * 8 + ht:le * 8 + ht + 1],
                            combb_s[:, th * 512:(th + 1) * 512],
                            ALU.add, ALU.mult, inc="yev")
                        R.e("vector", "tensor_add",
                            xsl(ht, th * 512, 512),
                            xsl(ht, th * 512, 512), tb_s[:], inc="yadd")
                    w2gate.append(R.n("mmy"))
                    w2gate.append(R.n("mmy"))
            yadd_done[(l, chunk)] = R.n("yadd")

        if lcs:
            emit_pre_x(*lcs[0])
            emit_pre_y1(*lcs[0])
            emit_pre_y2(*lcs[0])
            for kt0 in range(NCT):
                emit_xh_kt(*lcs[0], kt0)
        for idx in range(len(lcs)):
            emit_experts(idx)

        yadd_total = R.n("yadd")
        mmy_total = R.n("mmy")

        if stage in ("p0", "l1", "p1"):
            R.w("sync", "yadd", yadd_total)
            R.w("sync", "p0ev", p0ev_total)
            R.e("sync", "dma_start",
                out_h.rearrange("(ct p) t -> p ct t", p=128),
                x_ffn[:].rearrange("p (ct t) -> p ct t", ct=NCT),
                inc="st0", incn=16)
            if stage == "l1":
                dbg_defs = [("d_rwb", rwb_s, [128, 128], BF16),
                            ("d_xhat", xhat_s, [128, 8192], BF16),
                            ("d_logit", logit_s, [128, 32], F32),
                            ("d_rank", rank_s, [128, 32], F32),
                            ("d_m", m_s, [128, 32], F32),
                            ("d_comb", combf_s, [128, 32], F32),
                            ("d_u", u_s, [128, 32], F32),
                            ("d_combb", combb_s, [128, 1024], BF16),
                            ("d_xbe", xbe_s, [128, 8192], BF16)]
                for nm, src_ap, shp, dt_ in dbg_defs:
                    hdl = nc.dram_tensor(nm, shp, dt_,
                                         kind="ExternalOutput").ap()
                    R.e("sync", "dma_start", hdl[:, :], src_ap[:, :],
                        inc="st0", incn=16)
            R.w("sync", "st0", R.n("st0"))
            with nc.Block() as block:
                def mk(eng_name):
                    def body(eng):
                        for op, args, kw, inci in R.progs[eng_name]:
                            ins = getattr(eng, op)(*args, **kw)
                            if inci is not None:
                                ins.then_inc(*inci)
                    return body
                block.sync(mk("sync"))
                block.tensor(mk("tensor"))
                block.vector(mk("vector"))
                block.scalar(mk("scalar"))
                block.gpsimd(mk("gpsimd"))
            aX.close()
            return nc

        c2 = Carver()
        g_s = c2.take(NCT * NT // 2, BF16)
        gv = g_s.rearrange("p (ct t) -> p ct t", ct=NCT)
        R.w("scalar", "yadd", yadd_total)
        R.w("scalar", "mmy", mmy_total)
        for kt in range(2 * NCT):
            R.e("scalar", "activation", g_s[:, kt * TC:(kt + 1) * TC],
                x_ffn[:, kt * TC:(kt + 1) * TC], AF.Silu,
                inc=("gsl" if kt == 2 * NCT - 1 else None))
        gsl_total = R.n("gsl")
        aX.close()

        xn_s = stack.enter_context(
            nc.sbuf_tensor("xn_s", [128, NCT * NT], BF16))
        wq_s = stack.enter_context(
            nc.sbuf_tensor("wq_s", [128, NCT * D], BF16))
        wk_s = stack.enter_context(
            nc.sbuf_tensor("wk_s", [128, NCT * D], BF16))
        wv_s = c2.take(NCT * D // 2, BF16)
        wo_s = c2.take(NCT * D // 2, BF16)
        wg1_s = c2.take(NCT * HD // 2, BF16)
        wg2_s = c2.take(512, BF16)
        cosb_s = c2.take(1024, BF16)
        sinb_s = c2.take(1024, BF16)
        xcm_s = c2.take(2 * 1024)
        outb_s = c2.take(2 * 1024)
        E_s = c2.take(512, BF16)
        T_s = c2.take(1024)
        qs_s = c2.take(1024)
        sig_s = c2.take(1024, BF16)
        sigq_s = c2.take(512, BF16)
        Qcos_s = c2.take(512, BF16)
        Qsin_s = c2.take(512, BF16)
        Kcos_s = c2.take(512, BF16)
        Ksin_s = c2.take(512, BF16)
        KcosT_s = c2.take(512, BF16)
        KsinT_s = c2.take(512, BF16)
        V_s = c2.take(512, BF16)
        A_s = c2.take(1024, BF16)
        gg_s = c2.take(1024)
        ms_s = c2.take(16)
        rstd_s = c2.take(16)
        g1_s = c2.take(64, BF16, parts=(0, 64))
        normed_s = c2.take(512, BF16)
        carry_s = c2.take(8)
        Scf_s = c2.take(512)
        Ssf_s = c2.take(512)
        Scb_s = c2.take(256, BF16)
        Ssb_s = c2.take(256, BF16)
        cols2_s = c2.take(32)
        gg2_s = qs_s
        x2_s = outb_s
        xn2_s = A_s


        R.w("vector", "yadd", yadd_total)
        R.w("vector", "gsl", gsl_total)
        R.e("vector", "memset", carry_s[:], 1.0)
        R.e("vector", "memset", Scf_s[:], 0.0)
        R.e("vector", "memset", Ssf_s[:], 0.0)
        R.e("vector", "memset", Scb_s[:], 0.0)
        R.e("vector", "memset", Ssb_s[:], 0.0, inc="init2")

        R.w("gpsimd", "yadd", yadd_total)
        R.w("gpsimd", "mmy", mmy_total)
        R.w("gpsimd", "gsl", gsl_total)
        for dst, src in [(wq_s, wq_h), (wk_s, wk_h), (wv_s, wv_h),
                         (wg1_s, wg1_h), (wg2_s, wg2_h), (wo_s, wo_h)]:
            R.e("gpsimd", "dma_start", dst[:], src[:], inc="awl", incn=16)
        awl_total = R.n("awl")
        R.w("vector", "awl", awl_total)
        for kt in range(NCT):
            R.e("vector", "tensor_scalar_mul", wo_s[:, kt * D:(kt + 1) * D],
                wo_s[:, kt * D:(kt + 1) * D], nw_s[:, kt:kt + 1], inc="wof")

        R.sec = "p2_ln"
        R.w("sync", "gsl", gsl_total)
        xnv = xn_s[:].rearrange("p (ct t) -> p ct t", ct=NCT)
        x2bufs = [outb_s, xcm_s]
        for tt in range(16):
            par = tt % 4
            x2c = x2bufs[par // 2]
            if tt >= 4:
                R.w("sync", f"p2xn{par}", tt // 4)
            R.e("sync", "dma_start",
                x2c[:, (par % 2) * D:(par % 2 + 1) * D],
                x_tm[tt * 128:(tt + 1) * 128, :], inc=f"p2ld{par}", incn=16)
            if tt >= 2:
                R.w("vector", "p2mm", tt - 1)
            emit_ln_tile(par % 2, x2c, xn2_s,
                         cols2_s[:, par * 8:par * 8 + 8], gg_s[:, 0:1024],
                         f"p2ld{par}", 16 * (tt // 4 + 1), "p2",
                         sfx=str(par))
            bA, bB = (pb[0], pb[1]) if tt % 2 == 0 else (pb[2], pb[3])
            R.w("tensor", f"p2xn{par}", tt // 4 + 1)
            if tt == 0:
                R.w("tensor", "hge", R.n("hge"))
                R.w("tensor", "yev", R.n("yev"))
                R.w("tensor", "xhat", R.n("xhat"))
                R.w("tensor", "lgev", R.n("lgev"))
                R.w("tensor", "cbe", R.n("cbe"))
            if tt >= 2:
                R.w("tensor", "p2ev", 2 * (tt - 1))
            for ct in range(NCT):
                bank = bA if ct < 4 else bB
                R.e("tensor", "matmul",
                    bank[:, (ct % 4) * 128:(ct % 4 + 1) * 128],
                    xn2_s[:, (par % 2) * D + ct * 128:
                          (par % 2) * D + (ct + 1) * 128],
                    idnb_s[:], start=True, stop=True,
                    inc=("p2mm" if ct == NCT - 1 else None))
            R.w("vector", "p2mm", tt + 1)
            for half in range(2):
                bank = (bA, bB)[half]
                R.e("vector", "tensor_copy",
                    xnv[:, half * 4:(half + 1) * 4, tt * 128:(tt + 1) * 128],
                    bank[:].rearrange("p (ct t) -> p ct t", ct=4), inc="p2ev")
        p2ev_total = R.n("p2ev")

        tpc = 0
        R.w("tensor", "awl", awl_total)
        R.w("tensor", "wof", R.n("wof"))
        R.w("tensor", "p2ev", p2ev_total)
        R.w("tensor", "init2", 1)

        def tp_batch(lhs_src, cols, dst, dst_off, nb):
            nonlocal tpc
            bank = pb[6 + tpc % 2]
            if tpc >= 2:
                R.w("tensor", "tpe", tpc - 1)
            for j in range(nb):
                R.e("tensor", "matmul", bank[:, j * 128:(j + 1) * 128],
                    lhs_src[:, cols[j] * 128:(cols[j] + 1) * 128], idnb_s[:],
                    start=True, stop=True,
                    inc=("tpm" if j == nb - 1 else None))
            R.w("vector", "tpm", R.n("tpm"))
            R.e("vector", "tensor_copy", dst[:, dst_off:dst_off + nb * 128],
                bank[:, 0:nb * 128], inc="tpe")
            tpc += 1

        def emitA(i):
            par = i % 2
            tok = i * CH
            if i >= 2:
                R.w("sync", "dvq", i - 1)
            R.e("sync", "dma_start",
                cosb_s[:, par * 1024:(par + 1) * 1024]
                .rearrange("p (ct t) -> p ct t", ct=NCT),
                cos_h.rearrange("(ct p) t -> p ct t", p=128)[:, :,
                                                            tok:tok + CH],
                inc=f"cs{par}", incn=16)
            R.e("sync", "dma_start",
                sinb_s[:, par * 1024:(par + 1) * 1024]
                .rearrange("p (ct t) -> p ct t", ct=NCT),
                sin_h.rearrange("(ct p) t -> p ct t", p=128)[:, :,
                                                            tok:tok + CH],
                inc=f"cs{par}", incn=16)
            if i >= 2:
                R.w("sync", "oev", i - 1)
            else:
                R.w("sync", "p2xn2", 4)
                R.w("sync", "p2xn3", 4)
            R.e("sync", "dma_start",
                xcm_s[:, par * 1024:(par + 1) * 1024]
                .rearrange("p (ct t) -> p ct t", ct=NCT),
                x_cm.rearrange("(ct p) t -> p ct t", p=128)[:, :,
                                                           tok:tok + CH],
                inc=f"xld{par}", incn=16)

            R.sec = f"kq{i}"
            R.w("tensor", "sgev", 2 * i)
            for ct in range(NCT):
                bank = pb[ct % 2]
                sl = (ct // 2) * 128
                for kt in range(NCT):
                    R.e("tensor", "matmul", bank[:, sl:sl + 128],
                        wk_s[:, kt * D + ct * 128:kt * D + (ct + 1) * 128],
                        xn_s[:, kt * NT + tok:kt * NT + tok + CH],
                        start=(kt == 0), stop=(kt == NCT - 1),
                        inc=("kmm" if ct == NCT - 1 and kt == NCT - 1
                             else None))
            R.w("scalar", "kmm", i + 1)
            R.w("scalar", "dvq", i)
            E_v = E_s.rearrange("p (c2 two t) -> p c2 two t", two=2, t=128)
            R.e("scalar", "activation", E_v[:, :, 0, :],
                pb[0][:, 0:512].rearrange("p (c2 t) -> p c2 t", t=128),
                AF.Exp)
            R.e("scalar", "activation", E_v[:, :, 1, :],
                pb[1][:, 0:512].rearrange("p (c2 t) -> p c2 t", t=128),
                AF.Exp, inc="eev")
            R.w("tensor", "eev", i + 1)
            for ct in range(NCT):
                bank = pb[ct % 2]
                sl = (ct // 2) * 128
                for kt in range(NCT):
                    R.e("tensor", "matmul", bank[:, sl:sl + 128],
                        wq_s[:, kt * D + ct * 128:kt * D + (ct + 1) * 128],
                        xn_s[:, kt * NT + tok:kt * NT + tok + CH],
                        start=(kt == 0), stop=(kt == NCT - 1),
                        inc=("qmm" if ct == NCT - 1 and kt == NCT - 1
                             else None))
            R.w("vector", "eev", i + 1)
            if i == 0:
                R.w("vector", "init2", 1)
            for ct in range(NCT):
                R.e("vector", "tensor_tensor_scan",
                    T_s[:, ct * 128:(ct + 1) * 128], onesf_s[:],
                    E_s[:, ct * 128:(ct + 1) * 128], carry_s[:, ct:ct + 1],
                    ALU.mult, ALU.add)
            R.e("vector", "tensor_copy", carry_s[:], T_s[:, 127::128],
                inc="scn")
            R.w("scalar", "scn", R.n("scn"))
            R.e("scalar", "activation", T_s[:], T_s[:], AF.Ln)
            R.e("scalar", "activation", T_s[:], T_s[:], AF.Exp, scale=-1.0,
                inc="rte")
            R.w("scalar", "qmm", i + 1)
            R.e("scalar", "activation", sigq_s[:, 0:512], pb[0][:, 0:512],
                AF.Sigmoid)
            R.e("scalar", "activation", sigq_s[:, 512:1024], pb[1][:, 0:512],
                AF.Sigmoid, inc="sqe")
            R.w("vector", "sqe", i + 1)
            qs_v = qs_s.rearrange("p (c2 two t) -> p c2 two t", two=2, t=128)
            R.e("vector", "tensor_mul", qs_v[:, :, 0, :],
                pb[0][:, 0:512].rearrange("p (c2 t) -> p c2 t", t=128),
                sigq_s[:, 0:512].rearrange("p (c2 t) -> p c2 t", t=128))
            R.e("vector", "tensor_mul", qs_v[:, :, 1, :],
                pb[1][:, 0:512].rearrange("p (c2 t) -> p c2 t", t=128),
                sigq_s[:, 512:1024].rearrange("p (c2 t) -> p c2 t", t=128))
            R.w("vector", "rte", i + 1)
            R.e("vector", "tensor_mul", qs_s[:], qs_s[:], T_s[:], inc="qse")
            R.sec = f"vg{i}"
            R.w("tensor", "qse", i + 1)
            for half in range(2):
                for kt in range(NCT):
                    R.e("tensor", "matmul", pb[half][:, 0:512],
                        xn_s[:, kt * NT + tok:kt * NT + tok + CH],
                        wv_s[:, kt * D + half * 512:
                             kt * D + (half + 1) * 512],
                        start=(kt == 0), stop=(kt == NCT - 1),
                        inc=("vmm" if kt == NCT - 1 else None))
            R.w("vector", "vmm", R.n("vmm"))
            R.e("vector", "tensor_copy", V_s[:, 0:512], pb[0][:, 0:512])
            R.e("vector", "tensor_copy", V_s[:, 512:1024], pb[1][:, 0:512],
                inc="vev")
            R.w("tensor", "vev", i + 1)
            for kt in range(NCT):
                R.e("tensor", "matmul", pb[0][0:64, 0:128],
                    wg1_s[:, kt * HD:(kt + 1) * HD],
                    xn_s[:, kt * NT + tok:kt * NT + tok + CH],
                    start=(kt == 0), stop=(kt == NCT - 1),
                    inc=("g1mm" if kt == NCT - 1 else None))
            R.w("vector", "g1mm", i + 1)
            R.e("vector", "tensor_copy", g1_s[:], pb[0][0:64, 0:128],
                inc="g1ev")
            R.w("tensor", "g1ev", i + 1)
            for half in range(2):
                R.e("tensor", "matmul", pb[half][:, 0:512], g1_s[:],
                    wg2_s[0:64, half * 512:(half + 1) * 512],
                    start=True, stop=True,
                    inc=("gtmm" if half == 1 else None))
            R.w("scalar", "gtmm", i + 1)
            R.e("scalar", "activation", sig_s[:, par * 1024:par * 1024 + 512],
                pb[0][:, 0:512], AF.Sigmoid)
            R.e("scalar", "activation",
                sig_s[:, par * 1024 + 512:par * 1024 + 1024],
                pb[1][:, 0:512], AF.Sigmoid, inc="sgev", incn=2)

        def emitB_head(i):
            nonlocal tpc
            par = i % 2
            R.w("vector", f"cs{par}", 32 * (i // 2 + 1))
            cbuf = cosb_s[:, par * 1024:(par + 1) * 1024]
            sbf = sinb_s[:, par * 1024:(par + 1) * 1024]
            R.e("vector", "tensor_mul", Qcos_s[:], qs_s[:], cbuf)
            R.e("vector", "tensor_mul", Qsin_s[:], qs_s[:], sbf)
            R.e("vector", "tensor_mul", Kcos_s[:], E_s[:], cbuf)
            R.e("vector", "tensor_mul", Ksin_s[:], E_s[:], sbf, inc="dvq")
            R.sec = f"tpA{i}"
            R.w("tensor", "dvq", i + 1)
            tp_batch(Kcos_s, [0, 1, 2, 3], KcosT_s, 0, 4)
            tp_batch(Kcos_s, [4, 5, 6, 7], KcosT_s, 512, 4)
            tp_batch(Ksin_s, [0, 1, 2, 3], KsinT_s, 0, 4)
            tp_batch(Ksin_s, [4, 5, 6, 7], KsinT_s, 512, 4)
            ktpe_done = R.n("tpe")
            A_v = A_s.rearrange("p (hh two t) -> p hh two t", two=2, t=128)
            mask_v = mask_s[:].rearrange("p (m t) -> p m t", t=128)
            for octave in range(2):
                cA, cB = tpc, tpc + 1
                if cA >= 2:
                    R.w("tensor", "tpe", cA - 1)
                if cB >= 2:
                    R.w("tensor", "tpe", cB - 1)
                for j in range(8):
                    h = 8 * octave + j
                    bank = pb[6 + j % 2]
                    sl = (j // 2) * 128
                    p0 = 64 * (h % 2)
                    c0 = (h // 2) * 128
                    R.e("tensor", "matmul", bank[:, sl:sl + 128],
                        Kcos_s[p0:p0 + 64, c0:c0 + 128],
                        Qcos_s[p0:p0 + 64, c0:c0 + 128],
                        start=True, stop=False)
                    R.e("tensor", "matmul", bank[:, sl:sl + 128],
                        Ksin_s[p0:p0 + 64, c0:c0 + 128],
                        Qsin_s[p0:p0 + 64, c0:c0 + 128],
                        start=False, stop=True,
                        inc=("tpm" if j >= 6 else None))
                tbase = R.n("tpm")
                R.w("vector", "tpm", tbase - 1)
                R.e("vector", "tensor_mul",
                    A_v[:, 4 * octave:4 * octave + 4, 0, :],
                    pb[6][:, 0:512].rearrange("p (m t) -> p m t", t=128),
                    mask_v, inc="tpe")
                R.w("vector", "tpm", tbase)
                R.e("vector", "tensor_mul",
                    A_v[:, 4 * octave:4 * octave + 4, 1, :],
                    pb[7][:, 0:512].rearrange("p (m t) -> p m t", t=128),
                    mask_v, inc="tpe")
                tpc += 2
            aev_done = R.n("tpe")
            R.sec = f"dS{i}"
            R.w("tensor", "tpe", ktpe_done)
            R.w("tensor", "scst", 2 * i)
            for h in range(NH):
                p0 = 64 * (h % 2)
                kc0 = (h // 2) * 128 + 64 * (h % 2)
                sc0 = (h // 2) * 64
                R.e("tensor", "matmul", pb[4][p0:p0 + 64, sc0:sc0 + 64],
                    KcosT_s[:, kc0:kc0 + 64], V_s[:, h * 64:(h + 1) * 64],
                    start=True, stop=True)
                R.e("tensor", "matmul", pb[5][p0:p0 + 64, sc0:sc0 + 64],
                    KsinT_s[:, kc0:kc0 + 64], V_s[:, h * 64:(h + 1) * 64],
                    start=True, stop=True,
                    inc=("dmm" if h == NH - 1 else None))
            R.sec = f"o{i}"
            R.w("tensor", "tpe", aev_done)
            R.w("tensor", "oev", i)
            for jj in range(NH):
                h = (jj // 2) + 8 * (jj % 2)
                p0 = 64 * (h % 2)
                c0 = (h // 2) * 128
                sc0 = (h // 2) * 64
                ob = pb[2 + h // 8]
                oc = (h % 8) * 64
                R.e("tensor", "matmul", ob[:, oc:oc + 64],
                    A_s[:, h * 128:(h + 1) * 128],
                    V_s[:, h * 64:(h + 1) * 64], start=True, stop=False)
                R.e("tensor", "matmul", ob[:, oc:oc + 64],
                    Qcos_s[p0:p0 + 64, c0:c0 + 128],
                    Scb_s[p0:p0 + 64, sc0:sc0 + 64], start=False, stop=False)
                R.e("tensor", "matmul", ob[:, oc:oc + 64],
                    Qsin_s[p0:p0 + 64, c0:c0 + 128],
                    Ssb_s[p0:p0 + 64, sc0:sc0 + 64], start=False, stop=True,
                    inc=("omm" if jj == NH - 1 else None))
            R.w("vector", "dmm", i + 1)
            R.e("vector", "tensor_add", Scf_s[:], Scf_s[:], pb[4][:, 0:512])
            R.e("vector", "tensor_add", Ssf_s[:], Ssf_s[:], pb[5][:, 0:512])
            R.w("vector", "omm", i + 1)
            R.e("vector", "tensor_copy", Scb_s[:], Scf_s[:])
            R.e("vector", "tensor_copy", Ssb_s[:], Ssf_s[:], inc="scst",
                incn=2)
            R.w("vector", "sgev", 2 * (i + 1))
            R.e("vector", "tensor_mul", gg_s[:, 0:512], pb[2][:, 0:512],
                sig_s[:, par * 1024:par * 1024 + 512])
            R.e("vector", "tensor_mul", gg_s[:, 512:1024], pb[3][:, 0:512],
                sig_s[:, par * 1024 + 512:par * 1024 + 1024],
                inc="ggv", incn=2)
            R.w("scalar", "ggv", R.n("ggv"))
            R.e("scalar", "activation", gg2_s[:], gg_s[:], AF.Square,
                inc="g2e")
            R.w("vector", "g2e", R.n("g2e"))
            for h in range(NH):
                R.e("vector", "reduce_sum", ms_s[:, h:h + 1],
                    gg2_s[:, h * 64:(h + 1) * 64], axis=AXX,
                    inc=("msv" if h == NH - 1 else None))
            R.w("scalar", "msv", R.n("msv"))
            R.e("scalar", "activation", ms_s[:], ms_s[:], AF.Ln,
                scale=1.0 / HD, bias=eps_s[:, 1:2])
            R.fence("scalar")
            R.e("scalar", "activation", rstd_s[:], ms_s[:], AF.Exp,
                scale=-0.5, inc="rse")
            R.w("vector", "rse", i + 1)
            for h in range(NH):
                R.e("vector", "tensor_single_scalar",
                    normed_s[:, h * 64:(h + 1) * 64],
                    gg_s[:, h * 64:(h + 1) * 64], rstd_s[:, h:h + 1],
                    ALU.mult, inc=("nrm" if h == NH - 1 else None))

        def emitB_tail(i):
            par = i % 2
            tok = i * CH
            R.sec = f"ntp{i}"
            R.w("tensor", "nrm", i + 1)
            tp_batch(normed_s, [0, 1, 2, 3], A_s, 0, 4)
            tp_batch(normed_s, [4, 5, 6, 7], A_s, 512, 4)
            R.sec = f"sp{i}"
            R.w("tensor", "tpe", R.n("tpe"))
            R.w("tensor", "ggv", R.n("ggv"))
            for ct in range(NCT):
                bank = pb[2 + ct % 2]
                sl = (ct // 2) * 128
                for kt in range(NCT):
                    R.e("tensor", "matmul", bank[:, sl:sl + 128],
                        wo_s[:, kt * D + ct * 128:kt * D + (ct + 1) * 128],
                        A_s[:, kt * 128:(kt + 1) * 128],
                        start=(kt == 0), stop=(kt == NCT - 1),
                        inc=("smm" if ct == NCT - 1 and kt == NCT - 1
                             else None))
            R.w("vector", "smm", i + 1)
            R.w("vector", f"xld{par}", 16 * (i // 2 + 1))
            if i >= 2:
                R.w("vector", f"st{par}", 16 * ((i - 2) // 2 + 1))
            gg_v = gg_s.rearrange("p (c2 two t) -> p c2 two t", two=2,
                                  t=128)
            for half in range(2):
                R.e("vector", "tensor_mul", gg_v[:, :, half, :],
                    pb[2 + half][:, 0:512]
                    .rearrange("p (ct t) -> p ct t", ct=4),
                    gv[:, half::2, tok:tok + CH])
            R.e("vector", "tensor_add",
                outb_s[:, par * 1024:(par + 1) * 1024], gg_s[:],
                xcm_s[:, par * 1024:(par + 1) * 1024], inc="oev")
            R.w("sync", "oev", i + 1)
            R.e("sync", "dma_start",
                out_h.rearrange("(ct p) t -> p ct t", p=128)[:, :,
                                                            tok:tok + CH],
                outb_s[:, par * 1024:(par + 1) * 1024]
                .rearrange("p (ct t) -> p ct t", ct=NCT),
                inc=f"st{par}", incn=16)

        emitA(0)
        for i in range(NCH):
            emitB_head(i)
            if i + 1 < NCH:
                emitA(i + 1)
            emitB_tail(i)

        R.w("sync", "st0", 16 * (NCH // 2))
        R.w("sync", "st1", 16 * (NCH // 2))

        globals()["TLABELS"] = list(R.tensor_labels)
        with nc.Block() as block:
            def mk(eng_name):
                def body(eng):
                    for op, args, kw, inci in R.progs[eng_name]:
                        ins = getattr(eng, op)(*args, **kw)
                        if inci is not None:
                            ins.then_inc(*inci)
                return body
            block.sync(mk("sync"))
            block.tensor(mk("tensor"))
            block.vector(mk("vector"))
            block.scalar(mk("scalar"))
            block.gpsimd(mk("gpsimd"))
    return nc


def host_prep(inputs):
    f32 = np.float32
    wq = np.asarray(inputs["L0_attn_wq"], f32)
    wk = np.asarray(inputs["L0_attn_wk"], f32)
    wv = np.asarray(inputs["L0_attn_wv"], f32)
    wo = np.asarray(inputs["L0_attn_wo"], f32)
    wg1 = np.asarray(inputs["L0_attn_wg1"], f32)
    wg2 = np.asarray(inputs["L0_attn_wg2"], f32)
    nw = np.asarray(inputs["L0_attn_norm_w"], f32)
    rw = np.asarray(inputs["L0_ffn_router_w"], f32)
    rb = np.asarray(inputs["L0_ffn_router_b"], f32)
    lng = np.asarray(inputs["L0_ffn_ln_g"], f32)
    lnb = np.asarray(inputs["L0_ffn_ln_b"], f32)
    w1 = np.asarray(inputs["L0_ffn_w1"], f32)
    b1 = np.asarray(inputs["L0_ffn_b1"], f32)
    w2 = np.asarray(inputs["L0_ffn_w2"], f32)
    b2 = np.asarray(inputs["L0_ffn_b2"], f32)

    wt = lambda w: np.ascontiguousarray(
        w.reshape(8, 128, -1).transpose(1, 0, 2).reshape(128, -1))
    common = {
        "wq_t": wt(wq), "wk_t": wt(wk), "wv_t": wt(wv), "wo_t": wt(wo),
        "wg1_t": wt(wg1),
        "wg2": np.ascontiguousarray(
            np.concatenate([wg2, np.zeros((64, D), np.float32)], axis=0)),
        "nw_t": np.ascontiguousarray(nw.reshape(8, 128).T),
        "rw_t": np.ascontiguousarray(
            rw.reshape(NL, 8, 128, 4).transpose(2, 0, 1, 3).reshape(128,
                                                                    128)),
        "rbb": np.ascontiguousarray(np.broadcast_to(
            np.broadcast_to(rb.reshape(NL, 1, 4), (NL, 8, 4)).reshape(1, 128),
            (128, 128))),
        "b1t": np.ascontiguousarray(
            b1.reshape(NL, NE, 32, 128).transpose(3, 0, 1, 2).reshape(128,
                                                                      512)),
        "b2t": np.ascontiguousarray(
            b2.reshape(NL, NE, 8, 128).transpose(3, 0, 1, 2).reshape(128,
                                                                     128)),
        "lng": np.ascontiguousarray(
            lng.reshape(NL, NE, 8, 128).transpose(3, 0, 1, 2).reshape(128,
                                                                      128)),
        "lnb": np.ascontiguousarray(
            lnb.reshape(NL, NE, 8, 128).transpose(3, 0, 1, 2).reshape(128,
                                                                      128)),
        "w1": w1, "w2": w2,
        "idnf": np.eye(128, dtype=f32),
        "idnb": np.eye(128, dtype=ml_dtypes.bfloat16),
    }
    th = 10000.0 ** (-2.0 / D * np.arange(D, dtype=np.float64))
    ang = th[:, None] * np.arange(NT, dtype=np.float64)[None, :]
    common["cosb"] = np.cos(ang).astype(ml_dtypes.bfloat16)
    common["sinb"] = np.sin(ang).astype(ml_dtypes.bfloat16)
    tri = (np.arange(128)[:, None] <= np.arange(128)[None, :])
    common["mask4"] = np.ascontiguousarray(
        np.tile(tri, (1, 4)).astype(ml_dtypes.bfloat16))
    oneh = np.zeros((128, 512), dtype=ml_dtypes.bfloat16)
    for e in range(4):
        oneh[e, e * 128:(e + 1) * 128] = 1.0
    common["oneh4"] = oneh
    return common


def kernel(**inputs):
    x = np.asarray(inputs["L0_attn_x"], dtype=np.float32)
    B = x.shape[0]
    common = host_prep(inputs)
    nc = build_program()
    in_maps = []
    for b in range(B):
        m = dict(common)
        m["x_tm"] = np.ascontiguousarray(x[b])
        m["x_cm"] = np.ascontiguousarray(x[b].T)
        in_maps.append(m)
    res = run_bass_kernel_spmd(nc, in_maps, core_ids=list(range(B)))
    out = np.stack([res.results[b]["out"].T for b in range(B)], axis=0)
    return out.astype(np.float32)
